# version 11
# baseline (speedup 1.0000x reference)
import numpy as np
from contextlib import ExitStack

import concourse.bass as bass
import concourse.mybir as mybir
from concourse.bass_utils import run_bass_kernel_spmd

F32 = mybir.dt.float32
BF16 = mybir.dt.bfloat16
AF = mybir.ActivationFunctionType
ALU = mybir.AluOpType
AX = mybir.AxisListType

T = 4096
D = 1024
FF = 4096
HC = 8
NE = 9
NQC = 8
NTT = 32
EPS = 1e-5
SCL = 1.0 / 32.0

DBG = False


class Sched:
    ENGS = ("tensor", "vector", "scalar", "gpsimd", "sync")

    def __init__(self, nc, es):
        self.nc = nc
        self.q = {e: [] for e in self.ENGS}
        self.cnt = {}
        self.h = {}
        self.es = es

    def sem(self, *names):
        for n in names:
            self.h[n] = self.es.enter_context(self.nc.semaphore(n))
            self.cnt[n] = 0

    def val(self, name):
        return self.cnt[name]

    def op(self, eng, fn, w=(), i=None):
        waits = [(self.h[s], v) for s, v in w if v > 0]
        tgt = None
        if i is not None:
            s, n = i
            self.cnt[s] += n
            tgt = (self.h[s], n)

        def emit(e, waits=waits, fn=fn, tgt=tgt):
            for hh, v in waits:
                e.wait_ge(hh, v)
            ins = fn(e)
            if tgt is not None:
                ins.then_inc(*tgt)
        self.q[eng].append(emit)
        return self.cnt[i[0]] if i is not None else None

    def wait(self, eng, *w):
        waits = [(self.h[s], v) for s, v in w if v > 0]

        def emit(e, waits=waits):
            for hh, v in waits:
                e.wait_ge(hh, v)
        self.q[eng].append(emit)

    def barrier(self, extra_sync_waits=()):
        tgt = self.cnt["bar"] + 5
        for eng in self.ENGS:
            if eng == "sync":
                self.wait("sync", *extra_sync_waits)
            self.op(eng, lambda e: e.nop(), i=("bar", 1))
        for eng in self.ENGS:
            self.wait(eng, ("bar", tgt))


def build_nc(debug=False):
    nc = bass.Bass("TRN2", target_bir_lowering=False)
    es = ExitStack()

    xin = nc.dram_tensor("x_in", [T, D], F32, kind="ExternalInput").ap()
    wq = nc.dram_tensor("wq", [D, D], F32, kind="ExternalInput").ap()
    wk = nc.dram_tensor("wk", [D, D], F32, kind="ExternalInput").ap()
    wv = nc.dram_tensor("wv", [D, D], F32, kind="ExternalInput").ap()
    gw = nc.dram_tensor("gw", [8, D], F32, kind="ExternalInput").ap()
    w1 = nc.dram_tensor("w1", [8, D, FF], F32, kind="ExternalInput").ap()
    b1v = nc.dram_tensor("b1v", [8, FF], F32, kind="ExternalInput").ap()
    w2 = nc.dram_tensor("w2", [8, FF, D], F32, kind="ExternalInput").ap()
    b2v = nc.dram_tensor("b2v", [8, D], F32, kind="ExternalInput").ap()
    sw1 = nc.dram_tensor("sw1", [D, FF], F32, kind="ExternalInput").ap()
    sb1v = nc.dram_tensor("sb1v", [FF], F32, kind="ExternalInput").ap()
    sw2 = nc.dram_tensor("sw2", [FF, D], F32, kind="ExternalInput").ap()
    sb2v = nc.dram_tensor("sb2v", [D], F32, kind="ExternalInput").ap()
    out = nc.dram_tensor("out", [T, D], F32, kind="ExternalOutput").ap()

    sk = "ExternalOutput" if debug else "Internal"
    qs_h = nc.dram_tensor("qs_h", [NTT, 128, HC * 128], BF16, kind=sk).ap()
    qs_l = nc.dram_tensor("qs_l", [NTT, 128, HC * 128], BF16, kind=sk).ap()
    ks_h = nc.dram_tensor("ks_h", [NQC, 128, HC * 512], BF16, kind=sk).ap()
    ks_l = nc.dram_tensor("ks_l", [NQC, 128, HC * 512], BF16, kind=sk).ap()
    vts_h = nc.dram_tensor("vts_h", [NTT, 128, D], BF16, kind=sk).ap()
    vts_l = nc.dram_tensor("vts_l", [NTT, 128, D], BF16, kind=sk).ap()
    x2_h = nc.dram_tensor("x2_h", [NQC, 128, HC * 512], BF16, kind=sk).ap()
    x2_l = nc.dram_tensor("x2_l", [NQC, 128, HC * 512], BF16, kind=sk).ap()
    x1s = nc.dram_tensor("x1s", [T, D], F32, kind=sk).ap()
    cmbs = nc.dram_tensor("cmbs", [128, NTT * 9], F32, kind=sk).ap() if debug else None
    w1s_h = nc.dram_tensor("w1s_h", [NE, 4, 128, HC * 1024], BF16, kind="Internal").ap()
    w1s_l = nc.dram_tensor("w1s_l", [NE, 4, 128, HC * 1024], BF16, kind="Internal").ap()
    w2s_h = nc.dram_tensor("w2s_h", [NE, 4, 2, 128, 8 * 512], BF16, kind="Internal").ap()
    w2s_l = nc.dram_tensor("w2s_l", [NE, 4, 2, 128, 8 * 512], BF16, kind="Internal").ap()

    sch = Sched(nc, es)
    sch.sem("tmm", "vec", "sca", "gpc", "bar")
    sch.sem("lx0", "lx1", "lst0", "lst1", "lq0", "lq1", "lk0", "lk1",
            "lvt0", "lvt1", "lvt2", "lvt3", "lx20", "lx21", "lx1c", "lw0", "lw1", "lw20",
            "lw21", "lsetup")
    sch.sem("sev", "sxlt", "sx1b0", "sx1b1", "ssw0", "ssw1", "sx1c")

    A = es.enter_context

    BP = A(nc.sbuf_tensor("BP", [128, 78848], BF16))
    FP = A(nc.sbuf_tensor("FP", [128, 8192], F32))

    def bsl(a, b):
        return BP[:, a:b]

    def fsl(a, b):
        return FP[:, a:b]

    identb = A(nc.sbuf_tensor("identb", [128, 128], BF16))
    identf = A(nc.sbuf_tensor("identf", [128, 128], F32))
    gwT_h = A(nc.sbuf_tensor("gwT_h", [128, HC * 8], BF16))
    gwT_l = A(nc.sbuf_tensor("gwT_l", [128, HC * 8], BF16))
    b1T = A(nc.sbuf_tensor("b1T", [128, NE * 32], F32))
    b2e = A(nc.sbuf_tensor("b2e", [9, D], F32))
    comb = A(nc.sbuf_tensor("comb", [128, NTT * 9], F32))
    stat = A(nc.sbuf_tensor("stat", [128, 64], F32))
    cx = A(nc.sbuf_tensor("cx", [128, 48], F32))
    cts = A(nc.sbuf_tensor("cts", [16, 128], F32))
    bpre = A(nc.sbuf_tensor("bpre", [32, 128], F32))
    gwpre = A(nc.sbuf_tensor("gwpre", [8, D], F32))

    wqkv = {
        "q": (bsl(0, 8192), bsl(8192, 16384)),
        "k": (bsl(16384, 24576), bsl(24576, 32768)),
        "v": (bsl(32768, 40960), bsl(40960, 49152)),
    }
    xh = [bsl(49152, 50176), bsl(50176, 51200)]
    xl = [bsl(51200, 52224), bsl(52224, 53248)]
    xlt_h = bsl(53248, 57344)
    xlt_l = bsl(57344, 61440)
    swh_a = [bsl(61440, 62464), bsl(62464, 63488)]
    swl_a = [bsl(63488, 64512), bsl(64512, 65536)]
    swh_c = [bsl(73728, 74752), bsl(74752, 75776)]
    swl_c = [bsl(75776, 76800), bsl(76800, 77824)]
    jnk = bsl(69632, 70656)
    ev_h = bsl(70656, 74752)
    ev_l = bsl(74752, 78848)
    stg_a = [fsl(0, 1024), fsl(1024, 2048)]
    stg_c = [fsl(5120, 6144), fsl(6144, 7168)]
    xt = [fsl(4096, 5120), fsl(5120, 6144)]
    xc = [fsl(6144, 7168), fsl(7168, 8192)]

    kk_h = [bsl(0, 4096), bsl(8192, 12288)]
    kk_l = [bsl(4096, 8192), bsl(12288, 16384)]
    qq_h = [bsl(16384, 17408), bsl(18432, 19456)]
    qq_l = [bsl(17408, 18432), bsl(19456, 20480)]
    aa_h = bsl(20480, 24576)
    aa_l = bsl(24576, 28672)
    vt_h = [bsl(28672, 29696), bsl(30720, 31744)]
    vt_l = [bsl(29696, 30720), bsl(31744, 32768)]
    at_sb = [bsl(32768, 33024), bsl(33024, 33280)]
    ss = fsl(0, 4096)

    w1h = [bsl(0, 8192), bsl(16384, 24576)]
    w1l = [bsl(8192, 16384), bsl(24576, 32768)]
    w2h = [bsl(32768, 36864), bsl(40960, 45056)]
    w2l = [bsl(36864, 40960), bsl(45056, 49152)]
    xth = [bsl(49152, 53248), bsl(57344, 61440)]
    xtl = [bsl(53248, 57344), bsl(61440, 65536)]
    hh = bsl(65536, 69632)
    hl = bsl(69632, 73728)
    x1t = fsl(0, 4096)
    h32 = fsl(4096, 4608)

    ps_f = [A(nc.psum_tensor(f"ps_f{i}", [128, 512], F32)) for i in range(4)]
    ps_w = A(nc.psum_tensor("ps_w", [128, 1024], F32))
    ps_b = [A(nc.psum_tensor(f"ps_b{i}", [128, 1024], BF16)) for i in range(2)]

    g1 = sch.op("gpsimd", lambda g: g.memset(identb[:], 0.0), i=("gpc", 1))
    sch.op("gpsimd", lambda g: g.affine_select(
        out=identb[:], in_=identb[:], compare_op=ALU.not_equal, fill=1.0,
        base=0, pattern=[[-1, 128]], channel_multiplier=1),
        w=[("gpc", g1)], i=("gpc", 1))
    g3 = sch.op("gpsimd", lambda g: g.memset(identf[:], 0.0), i=("gpc", 1))
    sch.op("gpsimd", lambda g: g.affine_select(
        out=identf[:], in_=identf[:], compare_op=ALU.not_equal, fill=1.0,
        base=0, pattern=[[-1, 128]], channel_multiplier=1),
        w=[("gpc", g3)], i=("gpc", 1))
    sch.op("gpsimd", lambda g: g.memset(
        comb[:].rearrange("p (t e) -> p t e", e=9)[:, :, 8:9], 1.0), i=("gpc", 1))
    gp_setup = sch.val("gpc")

    sch.op("sync", lambda e: e.dma_start(out=b2e[0:8, :], in_=b2v), i=("lsetup", 16))
    sch.op("sync", lambda e: e.dma_start(out=b2e[8:9, :], in_=sb2v[None, :]), i=("lsetup", 16))
    sch.op("sync", lambda e: e.dma_start(out=gwpre[:], in_=gw), i=("lsetup", 16))
    gw_ld = sch.val("lsetup")

    for hc in range(HC):
        tv = sch.op("tensor", lambda e, hc=hc: e.transpose(
            ps_f[0][:, 0:8], gwpre[:, hc * 128:(hc + 1) * 128], identf[0:8, 0:8]),
            w=[("lsetup", gw_ld), ("gpc", gp_setup),
               ("sca", sch.val("sca")), ("vec", sch.val("vec"))], i=("tmm", 1))
        sv = sch.op("scalar", lambda e, hc=hc: e.copy(
            gwT_h[:, hc * 8:(hc + 1) * 8], ps_f[0][:, 0:8]),
            w=[("tmm", tv)], i=("sca", 1))
        sch.op("vector", lambda e, hc=hc: e.tensor_sub(
            gwT_l[:, hc * 8:(hc + 1) * 8], ps_f[0][:, 0:8],
            gwT_h[:, hc * 8:(hc + 1) * 8]),
            w=[("sca", sv)], i=("vec", 1))

    for e_i in range(NE):
        src = b1v[e_i] if e_i < 8 else sb1v
        ld = sch.op("sync", lambda e, src=src: e.dma_start(
            out=bpre[:], in_=src.rearrange("(fc p) -> fc p", p=128)),
            w=[("tmm", sch.val("tmm"))], i=("lsetup", 16))
        tv = sch.op("tensor", lambda e: e.transpose(
            ps_f[1][:, 0:32], bpre[:], identf[0:32, 0:32]),
            w=[("lsetup", ld), ("vec", sch.val("vec"))], i=("tmm", 1))
        sch.op("vector", lambda e, e_i=e_i: e.tensor_copy(
            b1T[:, e_i * 32:(e_i + 1) * 32], ps_f[1][:, 0:32]),
            w=[("tmm", tv)], i=("vec", 1))

    _spl = {"last_vec": [0, 0], "last_store": [0, 0], "par": 0, "stg": None,
            "swh": None, "swl": None}

    def split_chunk(src_ap, ncols, dst_hi, dst_lo, store=None):
        par = _spl["par"]
        _spl["par"] ^= 1
        stg = _spl["stg"]
        lsem, ssem = f"lst{par}", f"ssw{par}"
        sch.op("sync", lambda e, s=src_ap, g=stg[par], n=ncols: e.dma_start(
            out=g[:, :n], in_=s),
            w=[("gpc", _spl["last_vec"][par])], i=(lsem, 16))
        lv = sch.val(lsem)
        if store is None:
            hi_t, lo_t = dst_hi, dst_lo
            gw_ = [(lsem, lv)]
        else:
            hi_t, lo_t = _spl["swh"][par][:, :ncols], _spl["swl"][par][:, :ncols]
            gw_ = [(lsem, lv), (ssem, _spl["last_store"][par])]
        gv = sch.op("gpsimd", lambda e, g=stg[par], n=ncols, h=hi_t: e.tensor_copy(
            h, g[:, :n]), w=gw_, i=("gpc", 1))
        vv = sch.op("gpsimd", lambda e, g=stg[par], n=ncols, h=hi_t, l=lo_t: e.tensor_sub(
            l, g[:, :n], h), w=[("gpc", gv)], i=("gpc", 1))
        _spl["last_vec"][par] = vv
        if store is not None:
            dh, dl = store
            if isinstance(dh, tuple):
                for jh in range(2):
                    sch.op("scalar", lambda e, d=dh[jh], h=hi_t, jh=jh: e.dma_start(
                        out=d, in_=h[:, jh * 512:(jh + 1) * 512]),
                        w=[("gpc", gv)], i=(ssem, 16))
                    sch.op("scalar", lambda e, d=dl[jh], l=lo_t, jh=jh: e.dma_start(
                        out=d, in_=l[:, jh * 512:(jh + 1) * 512]),
                        w=[("gpc", vv)], i=(ssem, 16))
            else:
                sch.op("scalar", lambda e, n=ncols, d=dh, h=hi_t: e.dma_start(
                    out=d, in_=h[:, :n]), w=[("gpc", gv)], i=(ssem, 16))
                sch.op("scalar", lambda e, n=ncols, d=dl, l=lo_t: e.dma_start(
                    out=d, in_=l[:, :n]), w=[("gpc", vv)], i=(ssem, 16))
            _spl["last_store"][par] = sch.val(ssem)

    _spl["stg"] = stg_a
    _spl["swh"], _spl["swl"] = swh_a, swl_a
    for nm, wsrc in (("q", wq), ("k", wk), ("v", wv)):
        hi_b, lo_b = wqkv[nm]
        for hc in range(HC):
            split_chunk(wsrc[hc * 128:(hc + 1) * 128, :], 1024,
                        hi_b[:, hc * 1024:(hc + 1) * 1024],
                        lo_b[:, hc * 1024:(hc + 1) * 1024])
    wqkv_done_gpc = sch.val("gpc")

    NITC = NE * 4
    split_jobs_by_it = [[] for _ in range(NITC)]
    for it_ in range(NITC):
        e_i, fq = divmod(it_, 4)
        w1src = w1[e_i] if e_i < 8 else sw1
        w2src = w2[e_i] if e_i < 8 else sw2
        for hc in range(HC):
            split_jobs_by_it[it_].append((
                w1src[hc * 128:(hc + 1) * 128, fq * 1024:(fq + 1) * 1024],
                (w1s_h[e_i, fq, :, hc * 1024:(hc + 1) * 1024],
                 w1s_l[e_i, fq, :, hc * 1024:(hc + 1) * 1024])))
        for fc in range(fq * 8, fq * 8 + 8):
            split_jobs_by_it[it_].append((
                w2src[fc * 128:(fc + 1) * 128, :],
                ((w2s_h[e_i, fq, 0, :, (fc % 8) * 512:(fc % 8) * 512 + 512],
                  w2s_h[e_i, fq, 1, :, (fc % 8) * 512:(fc % 8) * 512 + 512]),
                 (w2s_l[e_i, fq, 0, :, (fc % 8) * 512:(fc % 8) * 512 + 512],
                  w2s_l[e_i, fq, 1, :, (fc % 8) * 512:(fc % 8) * 512 + 512]))))
    split_marks = [None] * NITC
    split_flat = [(it_, src, store) for it_ in range(NITC)
                  for src, store in split_jobs_by_it[it_]]
    split_pos = [0]

    def pump_jobs(n):
        for _ in range(n):
            if split_pos[0] >= len(split_flat):
                return
            it_, src, store = split_flat[split_pos[0]]
            split_chunk(src, 1024, None, None, store=store)
            split_pos[0] += 1
            if (split_pos[0] >= len(split_flat)
                    or split_flat[split_pos[0]][0] != it_):
                split_marks[it_] = (sch.val("ssw0"), sch.val("ssw1"))

    def ensure_pumped(it_):
        while it_ < NITC and split_marks[it_] is None:
            pump_jobs(1)

    ln_prev_s3c = [0, 0]
    xhl_free_tmm = [0, 0]

    def emit_ln(par, xtw=0):
        base = 24 * par
        s1, mn, ss_, va, sd, r0, t_, rs = (stat[:, base + i:base + i + 1] for i in range(8))
        xtp, xcp, xhp, xlp = xt[par], xc[par], xh[par], xl[par]
        v1 = sch.op("vector", lambda e: e.reduce_sum(s1, xtp, axis=AX.X),
                    w=[("sca", ln_prev_s3c[par]), ("vec", xtw)], i=("vec", 1))
        v2 = sch.op("vector", lambda e: e.tensor_scalar_mul(mn, s1, 1.0 / D),
                    w=[("vec", v1)], i=("vec", 1))
        v3 = sch.op("vector", lambda e: e.tensor_scalar_sub(xcp, xtp, mn),
                    w=[("vec", v2)], i=("vec", 1))
        s1c = sch.op("scalar", lambda e: e.activation(
            jnk, xcp, AF.Square, accum_out=ss_), w=[("vec", v3)], i=("sca", 1))
        v4 = sch.op("vector", lambda e: e.tensor_scalar(
            va, ss_, 1.0 / D, EPS, ALU.mult, ALU.add), w=[("sca", s1c)], i=("vec", 1))
        s2c = sch.op("scalar", lambda e: e.activation(sd, va, AF.Sqrt),
                     w=[("vec", v4)], i=("sca", 1))
        n1 = sch.op("vector", lambda e: e.reciprocal(r0, sd), w=[("sca", s2c)], i=("vec", 1))
        n2 = sch.op("vector", lambda e: e.tensor_mul(t_, r0, r0), w=[("vec", n1)], i=("vec", 1))
        n3 = sch.op("vector", lambda e: e.tensor_mul(t_, t_, va), w=[("vec", n2)], i=("vec", 1))
        n4 = sch.op("vector", lambda e: e.tensor_scalar(
            t_, t_, -0.5, 1.5, ALU.mult, ALU.add), w=[("vec", n3)], i=("vec", 1))
        v6 = sch.op("vector", lambda e: e.tensor_mul(rs, r0, t_), w=[("vec", n4)], i=("vec", 1))
        s3c = sch.op("scalar", lambda e: e.activation(
            xhp, xcp, AF.Identity, scale=rs),
            w=[("vec", v6), ("tmm", xhl_free_tmm[par])], i=("sca", 1))
        v7 = sch.op("vector", lambda e: e.tensor_scalar_mul(xcp, xcp, rs),
                    w=[("vec", v6), ("sca", s3c)], i=("vec", 1))
        v8 = sch.op("vector", lambda e: e.tensor_sub(xlp, xcp, xhp),
                    w=[("sca", s3c), ("vec", v7), ("tmm", xhl_free_tmm[par])],
                    i=("vec", 1))
        ln_prev_s3c[par] = s3c
        return v8, s3c

    trp_evac = [0, 0]

    def emit_tr_to_fm(tile_in_chunk, par, vdone, sdone, xlt_war=0):
        tp = tile_in_chunk % 2
        last_t = None
        for hc in range(HC):
            w_ = [("sca", sdone), ("vec", trp_evac[tp])] if hc == 0 else []
            last_t = sch.op("tensor", lambda e, hc=hc, tp=tp, p=par: e.transpose(
                ps_b[tp][:, hc * 128:(hc + 1) * 128],
                xh[p][:, hc * 128:(hc + 1) * 128], identb[:]), w=w_, i=("tmm", 1))
        vh = sch.op("vector", lambda e, tp=tp, ti=tile_in_chunk: e.tensor_copy(
            xlt_h.rearrange("p (hc t) -> p hc t", hc=HC)[:, :, ti * 128:(ti + 1) * 128],
            ps_b[tp][:].rearrange("p (hc t) -> p hc t", hc=HC)),
            w=[("tmm", max(last_t, xlt_war))], i=("vec", 1))
        for hc in range(HC):
            w_ = [("vec", max(vh, vdone))] if hc == 0 else []
            last_t = sch.op("tensor", lambda e, hc=hc, tp=tp, p=par: e.transpose(
                ps_b[tp][:, hc * 128:(hc + 1) * 128],
                xl[p][:, hc * 128:(hc + 1) * 128], identb[:]), w=w_, i=("tmm", 1))
        vl = sch.op("vector", lambda e, tp=tp, ti=tile_in_chunk: e.tensor_copy(
            xlt_l.rearrange("p (hc t) -> p hc t", hc=HC)[:, :, ti * 128:(ti + 1) * 128],
            ps_b[tp][:].rearrange("p (hc t) -> p hc t", hc=HC)),
            w=[("tmm", max(last_t, xlt_war))], i=("vec", 1))
        trp_evac[tp] = vl
        xhl_free_tmm[par] = last_t
        return vl

    qk_guard = [(0, 0), (0, 0)]
    v_guard = [(0, 0), (0, 0)]
    ev_store_guard = 0
    xt_free_vec = [0, 0]
    xlt_war_tmm = 0

    for qc in range(NQC):
        for tt in range(4):
            p_ = tt % 2
            g = qc * 4 + tt
            ld = sch.op("sync", lambda e, g=g, p=p_: e.dma_start(
                out=xt[p], in_=xin[g * 128:(g + 1) * 128, :]),
                w=[("vec", xt_free_vec[p_])], i=(f"lx{p_}", 16))
            sch.wait("vector", (f"lx{p_}", ld))
            if qc == 0 and tt == 0:
                sch.wait("tensor", ("gpc", wqkv_done_gpc))
            vdone, sdone = emit_ln(p_)
            xt_free_vec[p_] = vdone
            emit_tr_to_fm(tt, p_, vdone, sdone, xlt_war=xlt_war_tmm)
            pump_jobs(2)
        xlt_ready = sch.val("vec")

        for nm, (dst_h, dst_l) in (("q", (qs_h, qs_l)), ("k", (ks_h, ks_l))):
            whi, wlo = wqkv[nm]
            for oc in range(HC):
                p_ = oc % 2
                sg, vg = qk_guard[p_]
                mmv = None
                for hc in range(HC):
                    st_h = whi[:, hc * 1024 + oc * 128: hc * 1024 + (oc + 1) * 128]
                    st_l = wlo[:, hc * 1024 + oc * 128: hc * 1024 + (oc + 1) * 128]
                    mv_h = xlt_h[:, hc * 512:(hc + 1) * 512]
                    mv_l = xlt_l[:, hc * 512:(hc + 1) * 512]
                    w_ = [("vec", max(xlt_ready, vg)), ("sca", sg)] if hc == 0 else []
                    sch.op("tensor", lambda e, a=st_h, b=mv_h, p=p_, f=(hc == 0): e.matmul(
                        ps_f[p][:], a, b, start=f, stop=False), w=w_)
                    sch.op("tensor", lambda e, a=st_h, b=mv_l, p=p_: e.matmul(
                        ps_f[p][:], a, b, start=False, stop=False))
                    mmv = sch.op("tensor", lambda e, a=st_l, b=mv_h, p=p_,
                                 l=(hc == HC - 1): e.matmul(
                        ps_f[p][:], a, b, start=False, stop=l),
                        i=("tmm", 1) if hc == HC - 1 else None)
                sv = sch.op("scalar", lambda e, oc=oc, p=p_: e.copy(
                    ev_h[:, oc * 512:(oc + 1) * 512], ps_f[p][:]),
                    w=[("tmm", mmv), ("sev", ev_store_guard)], i=("sca", 1))
                vv = sch.op("vector", lambda e, oc=oc, p=p_: e.tensor_sub(
                    ev_l[:, oc * 512:(oc + 1) * 512], ps_f[p][:],
                    ev_h[:, oc * 512:(oc + 1) * 512]),
                    w=[("sca", sv), ("sev", ev_store_guard)], i=("vec", 1))
                qk_guard[p_] = (sv, vv)
            if nm == "q":
                oap_h = dst_h[qc * 4:(qc + 1) * 4].rearrange(
                    "q p (hc t) -> p hc q t", t=128)
                oap_l = dst_l[qc * 4:(qc + 1) * 4].rearrange(
                    "q p (hc t) -> p hc q t", t=128)
                iap_h = ev_h.rearrange("p (hc q t) -> p hc q t", q=4, t=128)
                iap_l = ev_l.rearrange("p (hc q t) -> p hc q t", q=4, t=128)
            else:
                oap_h = dst_h[qc]
                oap_l = dst_l[qc]
                iap_h = ev_h
                iap_l = ev_l
            sch.op("scalar", lambda e, d=oap_h, s=iap_h: e.dma_start(
                out=d, in_=s),
                w=[("sca", max(qk_guard[0][0], qk_guard[1][0]))], i=("sev", 16))
            sch.op("scalar", lambda e, d=oap_l, s=iap_l: e.dma_start(
                out=d, in_=s),
                w=[("vec", max(qk_guard[0][1], qk_guard[1][1]))], i=("sev", 16))
            ev_store_guard = sch.val("sev")

        whi, wlo = wqkv["v"]
        for tt in range(4):
            for jh in range(2):
                p_ = (tt * 2 + jh) % 2
                sg, vg = v_guard[p_]
                mmv = None
                for hc in range(HC):
                    st_h = xlt_h[:, hc * 512 + tt * 128: hc * 512 + (tt + 1) * 128]
                    st_l = xlt_l[:, hc * 512 + tt * 128: hc * 512 + (tt + 1) * 128]
                    mv_h = whi[:, hc * 1024 + jh * 512: hc * 1024 + (jh + 1) * 512]
                    mv_l = wlo[:, hc * 1024 + jh * 512: hc * 1024 + (jh + 1) * 512]
                    w_ = [("sca", sg), ("vec", vg)] if hc == 0 else []
                    sch.op("tensor", lambda e, a=st_h, b=mv_h, p=p_, f=(hc == 0): e.matmul(
                        ps_f[2 + p][:], a, b, start=f, stop=False), w=w_)
                    sch.op("tensor", lambda e, a=st_h, b=mv_l, p=p_: e.matmul(
                        ps_f[2 + p][:], a, b, start=False, stop=False))
                    mmv = sch.op("tensor", lambda e, a=st_l, b=mv_h, p=p_,
                                 l=(hc == HC - 1): e.matmul(
                        ps_f[2 + p][:], a, b, start=False, stop=l),
                        i=("tmm", 1) if hc == HC - 1 else None)
                sv = sch.op("scalar", lambda e, tt=tt, jh=jh, p=p_: e.copy(
                    ev_h[:, tt * 1024 + jh * 512: tt * 1024 + (jh + 1) * 512],
                    ps_f[2 + p][:]),
                    w=[("tmm", mmv), ("sev", ev_store_guard)], i=("sca", 1))
                vv = sch.op("vector", lambda e, tt=tt, jh=jh, p=p_: e.tensor_sub(
                    ev_l[:, tt * 1024 + jh * 512: tt * 1024 + (jh + 1) * 512],
                    ps_f[2 + p][:],
                    ev_h[:, tt * 1024 + jh * 512: tt * 1024 + (jh + 1) * 512]),
                    w=[("sca", sv), ("sev", ev_store_guard)], i=("vec", 1))
                v_guard[p_] = (sv, vv)
        sch.op("scalar", lambda e, qc=qc: e.dma_start(
            out=vts_h[qc * 4:(qc + 1) * 4, :, :].rearrange("c p o -> p c o"),
            in_=ev_h.rearrange("p (c o) -> p c o", c=4)),
            w=[("sca", max(v_guard[0][0], v_guard[1][0]))], i=("sev", 16))
        sch.op("scalar", lambda e, qc=qc: e.dma_start(
            out=vts_l[qc * 4:(qc + 1) * 4, :, :].rearrange("c p o -> p c o"),
            in_=ev_l.rearrange("p (c o) -> p c o", c=4)),
            w=[("vec", max(v_guard[0][1], v_guard[1][1]))], i=("sev", 16))
        ev_store_guard = sch.val("sev")
        xlt_war_tmm = sch.val("tmm")

    ensure_pumped(1)

    sch.barrier(extra_sync_waits=[("sev", sch.val("sev")),
                                  ("ssw0", sch.val("ssw0")),
                                  ("ssw1", sch.val("ssw1"))])

    vtb_h = [bsl(70656 + s * 2048, 70656 + s * 2048 + 1024) for s in range(4)]
    vtb_l = [bsl(70656 + s * 2048 + 1024, 70656 + s * 2048 + 2048) for s in range(4)]

    kq_free = [0, 0]
    qq_free = [0, 0]
    vt_slot_free = [0, 0, 0, 0]
    s_guard = [0, 0]
    s_vec_guard = 0
    at_guard = [0, 0]
    av_guard = 0
    aa_guard_tmm = 0
    x1b_store = [0, 0]
    xlt_free = 0
    gate_stat_sca = 0
    ss_free_vec = 0
    trb_guard = [0, 0]

    nm_ = stat[:, 40:41]
    se_ = stat[:, 41:42]
    ra_ = stat[:, 42:43]
    pmax = stat[:, 48:56]

    def b_scores(qb):
        nonlocal s_vec_guard, ss_free_vec
        qpar = qb % 2
        sch.op("sync", lambda e, qb=qb, p=qpar: e.dma_start(
            out=qq_h[p], in_=qs_h[qb]),
            w=[("tmm", qq_free[qpar])], i=(f"lq{qpar}", 16))
        sch.op("sync", lambda e, qb=qb, p=qpar: e.dma_start(
            out=qq_l[p], in_=qs_l[qb]),
            i=(f"lq{qpar}", 16))
        lqv = sch.val(f"lq{qpar}")
        sch.op("sync", lambda e, qb=qb, p=qpar: e.dma_start(
            out=xt[p], in_=xin[qb * 128:(qb + 1) * 128, :]),
            w=[(f"sx1b{qpar}", x1b_store[qpar]), ("vec", xt_free_vec[qpar])],
            i=(f"lx{qpar}", 16))
        lxv = sch.val(f"lx{qpar}")

        pm_prev = 0
        for kb in range(8):
            kp = kb % 2
            sch.op("sync", lambda e, kb=kb, p=kp: e.dma_start(
                out=kk_h[p], in_=ks_h[kb]),
                w=[("tmm", kq_free[kp])], i=(f"lk{kp}", 16))
            sch.op("sync", lambda e, kb=kb, p=kp: e.dma_start(
                out=kk_l[p], in_=ks_l[kb]),
                i=(f"lk{kp}", 16))
            lkv = sch.val(f"lk{kp}")
            sp = kb % 2
            mmv = None
            for hc in range(HC):
                st_h = qq_h[qpar][:, hc * 128:(hc + 1) * 128]
                st_l = qq_l[qpar][:, hc * 128:(hc + 1) * 128]
                mv_h = kk_h[kp][:, hc * 512:(hc + 1) * 512]
                mv_l = kk_l[kp][:, hc * 512:(hc + 1) * 512]
                w_ = ([(f"lk{kp}", lkv), (f"lq{qpar}", lqv),
                       ("sca", s_guard[sp]), ("vec", s_vec_guard)] if hc == 0 else [])
                sch.op("tensor", lambda e, a=st_h, b=mv_h, p=sp, f=(hc == 0): e.matmul(
                    ps_f[p][:], a, b, start=f, stop=False), w=w_)
                sch.op("tensor", lambda e, a=st_h, b=mv_l, p=sp: e.matmul(
                    ps_f[p][:], a, b, start=False, stop=False))
                mmv = sch.op("tensor", lambda e, a=st_l, b=mv_h, p=sp,
                             l=(hc == HC - 1): e.matmul(
                    ps_f[p][:], a, b, start=False, stop=l),
                    i=("tmm", 1) if hc == HC - 1 else None)
            sv = sch.op("scalar", lambda e, kb=kb, p=sp: e.activation(
                ss[:, kb * 512:(kb + 1) * 512], ps_f[p][:], AF.Identity, scale=SCL),
                w=[("tmm", mmv), ("vec", ss_free_vec)], i=("sca", 1))
            pm = sch.op("vector", lambda e, kb=kb, p=sp: e.reduce_max(
                pmax[:, kb:kb + 1], ps_f[p][:], axis=AX.X),
                w=[("tmm", mmv), ("sca", sv), ("vec", pm_prev)], i=("vec", 1))
            pm_prev = pm
            s_guard[sp] = sv
            s_vec_guard = max(s_vec_guard, pm)
            kq_free[kp] = mmv
        qq_free[qpar] = sch.val("tmm")
        return {"qb": qb, "qpar": qpar, "qc": qb // 4, "tt": qb % 4,
                "lxv": lxv, "pm_last": pm_prev,
                "sv_last": (s_guard[0], s_guard[1])}

    def b_softmax(st):
        v1a = sch.op("vector", lambda e: e.reduce_max(nm_, pmax, axis=AX.X, negate=True),
                     w=[("vec", st["pm_last"]), ("tmm", aa_guard_tmm),
                        ("sca", gate_stat_sca)], i=("vec", 1))
        v1 = sch.op("vector", lambda e: e.tensor_scalar_mul(nm_, nm_, SCL),
                    w=[("vec", v1a)], i=("vec", 1))
        s1c = sch.op("scalar", lambda e: e.activation(
            aa_h, ss, AF.Exp, bias=nm_, accum_out=se_),
            w=[("vec", v1), ("sca", max(st["sv_last"]))], i=("sca", 1))
        s2c = sch.op("scalar", lambda e: e.activation(
            ss, ss, AF.Exp, bias=nm_), i=("sca", 1))
        v2 = sch.op("vector", lambda e: e.tensor_sub(aa_l, ss, aa_h),
                    w=[("sca", s2c)], i=("vec", 1))
        v3 = sch.op("vector", lambda e: e.reciprocal(ra_, se_),
                    w=[("sca", s1c), ("vec", v2)], i=("vec", 1))
        st["aa_ready"] = v2
        st["recip_ready"] = v3

    def b_av(st):
        nonlocal aa_guard_tmm, ss_free_vec
        aa_ready = st["aa_ready"]
        lvv = [0] * NTT
        ev_done = [0] * NTT
        avm_done = [0] * NTT

        def vt_load(kc):
            s_ = kc % 4
            war = avm_done[kc - 4] if kc >= 4 else vt_slot_free[s_]
            sch.op("sync", lambda e, kc=kc, s=s_: e.dma_start(
                out=vtb_h[s], in_=vts_h[kc]),
                w=[("tmm", war)], i=(f"lvt{s_}", 16))
            sch.op("sync", lambda e, kc=kc, s=s_: e.dma_start(
                out=vtb_l[s], in_=vts_l[kc]), i=(f"lvt{s_}", 16))
            lvv[kc] = sch.val(f"lvt{s_}")

        def tr(kc):
            ap_ = kc % 2
            bank_free = ev_done[kc - 2] if kc >= 2 else max(at_guard[ap_], trb_guard[ap_])
            sch.op("tensor", lambda e, kc=kc, p=ap_: e.transpose(
                ps_b[p][:, 0:128], aa_h[:, kc * 128:(kc + 1) * 128], identb[:]),
                w=[("vec", max(aa_ready, bank_free))])
            trv = sch.op("tensor", lambda e, kc=kc, p=ap_: e.transpose(
                ps_b[p][:, 128:256], aa_l[:, kc * 128:(kc + 1) * 128], identb[:]),
                i=("tmm", 1))
            slot_free = avm_done[kc - 2] if kc >= 2 else 0
            ev_done[kc] = sch.op("vector", lambda e, p=ap_: e.tensor_copy(
                at_sb[p], ps_b[p][:, 0:256]),
                w=[("tmm", max(trv, slot_free))], i=("vec", 1))

        def avmm(kc):
            ap_ = kc % 2
            s_ = kc % 4
            last = None
            for jh in range(2):
                a_h = at_sb[ap_][:, 0:128]
                a_l = at_sb[ap_][:, 128:256]
                m_h = vtb_h[s_][:, jh * 512:(jh + 1) * 512]
                m_l = vtb_l[s_][:, jh * 512:(jh + 1) * 512]
                w_ = []
                if jh == 0:
                    w_ = [("vec", max(ev_done[kc], av_guard) if kc == 0 else ev_done[kc]),
                          (f"lvt{s_}", lvv[kc])]
                f_ = (kc == 0)
                l_ = (kc == NTT - 1)
                sch.op("tensor", lambda e, a=a_h, b=m_h, jh=jh, f=f_: e.matmul(
                    ps_w[:, jh * 512:(jh + 1) * 512], a, b, start=f, stop=False), w=w_)
                sch.op("tensor", lambda e, a=a_h, b=m_l, jh=jh: e.matmul(
                    ps_w[:, jh * 512:(jh + 1) * 512], a, b, start=False, stop=False))
                last = sch.op("tensor", lambda e, a=a_l, b=m_h, jh=jh, l=l_: e.matmul(
                    ps_w[:, jh * 512:(jh + 1) * 512], a, b, start=False, stop=l),
                    i=("tmm", 1) if jh == 1 else None)
            avm_done[kc] = last

        for kc in range(3):
            vt_load(kc)
        tr(0)
        for kc in range(NTT):
            if kc + 3 < NTT:
                vt_load(kc + 3)
            if kc + 1 < NTT:
                tr(kc + 1)
            avmm(kc)
        at_guard[0] = ev_done[NTT - 2]
        at_guard[1] = ev_done[NTT - 1]
        trb_guard[0] = ev_done[NTT - 2]
        trb_guard[1] = ev_done[NTT - 1]
        for s_ in range(4):
            vt_slot_free[s_] = avm_done[NTT - 4 + s_]
        st["av_done"] = avm_done[NTT - 1]
        aa_guard_tmm = avm_done[NTT - 1]
        ss_free_vec = st["aa_ready"]

    def b_epilogue(st):
        nonlocal av_guard, xlt_free, gate_stat_sca, s_vec_guard
        qb, qpar, qc, tt = st["qb"], st["qpar"], st["qc"], st["tt"]
        v4 = sch.op("vector", lambda e, p=qpar: e.scalar_tensor_tensor(
            xt[p], ps_w[:, 0:1024], ra_, xt[p], ALU.mult, ALU.add),
            w=[("tmm", st["av_done"]), ("vec", st["recip_ready"]),
               (f"lx{qpar}", st["lxv"])], i=("vec", 1))
        av_guard = v4
        vdone, sdone = emit_ln(qpar, xtw=v4)
        for hc in range(HC):
            tp2 = hc % 2
            w_ = [("vec", trb_guard[tp2])]
            if hc == 0:
                w_ += [("sca", sdone), ("sxlt", xlt_free)]
            elif hc == 1:
                w_ += [("vec", vdone), ("sxlt", xlt_free)]
            sch.op("tensor", lambda e, hc=hc, p=tp2, q=qpar: e.transpose(
                ps_b[p][:, 0:128], xh[q][:, hc * 128:(hc + 1) * 128], identb[:]),
                w=w_)
            tv2 = sch.op("tensor", lambda e, hc=hc, p=tp2, q=qpar: e.transpose(
                ps_b[p][:, 128:256], xl[q][:, hc * 128:(hc + 1) * 128], identb[:]),
                w=[("vec", vdone)] if hc == 0 else [], i=("tmm", 1))
            sch.op("vector", lambda e, hc=hc, ti=tt, p=tp2: e.tensor_copy(
                xlt_h[:, hc * 512 + ti * 128: hc * 512 + (ti + 1) * 128],
                ps_b[p][:, 0:128]), w=[("tmm", tv2)], i=("vec", 1))
            vl2 = sch.op("vector", lambda e, hc=hc, ti=tt, p=tp2: e.tensor_copy(
                xlt_l[:, hc * 512 + ti * 128: hc * 512 + (ti + 1) * 128],
                ps_b[p][:, 128:256]), w=[("tmm", tv2)], i=("vec", 1))
            trb_guard[tp2] = vl2
        at_guard[0] = trb_guard[0]
        at_guard[1] = trb_guard[1]
        x2_evac = sch.val("vec")
        xhl_free_tmm[qpar] = sch.val("tmm")

        gmm = None
        for hc in range(HC):
            st_h = xlt_h[:, hc * 512 + tt * 128: hc * 512 + (tt + 1) * 128]
            st_l = xlt_l[:, hc * 512 + tt * 128: hc * 512 + (tt + 1) * 128]
            w_ = ([("vec", x2_evac), ("sca", s_guard[0])] if hc == 0 else [])
            sch.op("tensor", lambda e, a=st_h, hc=hc: e.matmul(
                ps_f[0][:, 0:8], a, gwT_h[:, hc * 8:(hc + 1) * 8],
                start=(hc == 0), stop=False), w=w_)
            sch.op("tensor", lambda e, a=st_h, hc=hc: e.matmul(
                ps_f[0][:, 0:8], a, gwT_l[:, hc * 8:(hc + 1) * 8],
                start=False, stop=False))
            gmm = sch.op("tensor", lambda e, a=st_l, hc=hc: e.matmul(
                ps_f[0][:, 0:8], a, gwT_h[:, hc * 8:(hc + 1) * 8],
                start=False, stop=(hc == HC - 1)),
                i=("tmm", 1) if hc == HC - 1 else None)

        m1 = stat[:, 44:45]
        m2 = stat[:, 45:46]
        dn = stat[:, 46:47]
        r2 = stat[:, 47:48]
        lc = cx[:, 0:8]
        ee = cx[:, 8:16]
        tmp = cx[:, 16:24]
        v5 = sch.op("vector", lambda e: e.reduce_max(m1, ps_f[0][:, 0:8], axis=AX.X),
                    w=[("tmm", gmm), ("sca", gate_stat_sca)], i=("vec", 1))
        v6 = sch.op("vector", lambda e: e.tensor_scalar_sub(lc, ps_f[0][:, 0:8], m1),
                    w=[("vec", v5)], i=("vec", 1))
        s3c = sch.op("scalar", lambda e: e.activation(ee, lc, AF.Exp),
                     w=[("vec", v6)], i=("sca", 1))
        v7 = sch.op("vector", lambda e: e.tensor_single_scalar(tmp, lc, 0.0, ALU.is_ge),
                    w=[("vec", v6)], i=("vec", 1))
        v8g = sch.op("vector", lambda e: e.scalar_tensor_tensor(
            tmp, tmp, -1e30, lc, ALU.mult, ALU.add), w=[("vec", v7)], i=("vec", 1))
        v9 = sch.op("vector", lambda e: e.reduce_max(m2, tmp, axis=AX.X),
                    w=[("vec", v8g)], i=("vec", 1))
        va_ = sch.op("vector", lambda e: e.tensor_single_scalar(tmp, lc, m2, ALU.is_ge),
                     w=[("vec", v9)], i=("vec", 1))
        s4c = sch.op("scalar", lambda e: e.activation(dn, m2, AF.Exp),
                     w=[("vec", v9)], i=("sca", 1))
        vb = sch.op("vector", lambda e: e.tensor_scalar_add(dn, dn, 1.0),
                    w=[("sca", s4c)], i=("vec", 1))
        vc = sch.op("vector", lambda e: e.reciprocal(r2, dn),
                    w=[("vec", vb)], i=("vec", 1))
        vd = sch.op("vector", lambda e: e.tensor_mul(tmp, tmp, ee),
                    w=[("sca", s3c), ("vec", va_)], i=("vec", 1))
        vee = sch.op("vector", lambda e, qb=qb: e.tensor_scalar_mul(
            comb[:, qb * 9: qb * 9 + 8], tmp, r2),
            w=[("vec", max(vc, vd))], i=("vec", 1))
        gate_stat_sca = s4c

        ct = sch.op("tensor", lambda e, qb=qb: e.transpose(
            ps_f[1][0:9, 0:128], comb[:, qb * 9: qb * 9 + 9], identf[:]),
            w=[("vec", vee), ("sca", s_guard[1]), ("gpc", gp_setup)], i=("tmm", 1))
        vf = sch.op("vector", lambda e: e.tensor_copy(cts[0:9, :], ps_f[1][0:9, 0:128]),
                    w=[("tmm", ct)], i=("vec", 1))
        bmm = None
        for jh in range(2):
            w_ = ([("vec", max(vf, av_guard)), ("lsetup", sch.val("lsetup"))]
                  if jh == 0 else [])
            bmm = sch.op("tensor", lambda e, jh=jh: e.matmul(
                ps_w[:, jh * 512:(jh + 1) * 512], cts[0:9, :],
                b2e[:, jh * 512:(jh + 1) * 512], start=True, stop=True),
                w=w_, i=("tmm", 1) if jh == 1 else None)
        vg_ = sch.op("vector", lambda e, p=qpar: e.tensor_add(
            xt[p], xt[p], ps_w[:, 0:1024]), w=[("tmm", bmm), ("vec", v4)], i=("vec", 1))
        av_guard = vg_
        xt_free_vec[qpar] = vg_
        sch.op("scalar", lambda e, qb=qb, p=qpar: e.dma_start(
            out=x1s[qb * 128:(qb + 1) * 128, :], in_=xt[p]),
            w=[("vec", vg_)], i=(f"sx1b{qpar}", 16))
        x1b_store[qpar] = sch.val(f"sx1b{qpar}")
        if tt == 3:
            sch.op("scalar", lambda e, qc=qc: e.dma_start(
                out=x2_h[qc], in_=xlt_h),
                w=[("vec", x2_evac), ("tmm", gmm)], i=("sxlt", 16))
            sch.op("scalar", lambda e, qc=qc: e.dma_start(
                out=x2_l[qc], in_=xlt_l),
                i=("sxlt", 16))
            xlt_free = sch.val("sxlt")
        s_guard[0] = 0
        s_guard[1] = 0
        s_vec_guard = max(s_vec_guard, vf)

    epi = None
    for qb in range(NTT):
        st = b_scores(qb)
        if epi is not None:
            b_epilogue(epi)
        b_softmax(st)
        b_av(st)
        epi = st
    b_epilogue(epi)

    if debug:
        sch.op("scalar", lambda e: e.dma_start(out=cmbs, in_=comb[:]),
               w=[("vec", sch.val("vec"))], i=("sxlt", 16))

    sch.barrier(extra_sync_waits=[("sx1b0", sch.val("sx1b0")),
                                  ("sx1b1", sch.val("sx1b1")),
                                  ("sxlt", sch.val("sxlt")),
                                  ("sev", sch.val("sev"))])

    ybank = [ps_f[2][:], ps_f[3][:], ps_w[:, 0:512], ps_w[:, 512:1024]]
    NIT = NE * 4
    w1_free = [0, 0]
    w1_ready = [0, 0]
    w2_free = [0, 0]
    w2_ready = [0, 0]
    x2_free = [0, 0]
    h_guard = [0, 0]
    y_guard = [0, 0, 0, 0]
    h_rdy_v = [0] * 8
    h_rdy_s = [0] * 8
    x1_st = 0
    h_cons = 0

    def w1_load_half(it, p_, half):
        e_i, fq = divmod(it, 4)
        ensure_pumped(it)
        mk = split_marks[it]
        srcs = w1s_h if half == 0 else w1s_l
        dst = w1h[p_] if half == 0 else w1l[p_]
        sch.op("sync", lambda e, e_i=e_i, fq=fq, d=dst: e.dma_start(
            out=d, in_=srcs[e_i, fq]),
            w=[("tmm", w1_free[p_]), ("ssw0", mk[0]), ("ssw1", mk[1])],
            i=(f"lw{p_}", 16))
        return sch.val(f"lw{p_}")

    def w2_load(it, jh):
        e_i, fq = divmod(it, 4)
        ensure_pumped(it)
        mk = split_marks[it]
        sch.op("sync", lambda e, e_i=e_i, fq=fq, jh=jh: e.dma_start(
            out=w2h[jh], in_=w2s_h[e_i, fq, jh]),
            w=[("tmm", w2_free[jh]), ("ssw0", mk[0]), ("ssw1", mk[1])],
            i=(f"lw2{jh}", 16))
        sch.op("sync", lambda e, e_i=e_i, fq=fq, jh=jh: e.dma_start(
            out=w2l[jh], in_=w2s_l[e_i, fq, jh]),
            i=(f"lw2{jh}", 16))
        return sch.val(f"lw2{jh}")

    w1_load_half(0, 0, 0)
    w1_ready[0] = w1_load_half(0, 0, 1)
    w1_load_half(1, 1, 0)
    w1_ready[1] = w1_load_half(1, 1, 1)
    w2_ready[0] = w2_load(0, 0)
    w2_ready[1] = w2_load(0, 1)

    _spl["stg"] = stg_c
    _spl["swh"], _spl["swl"] = swh_c, swl_c
    for it in range(NIT):
        e_i, fq = divmod(it, 4)
        wp = it % 2
        last_it = it == NIT - 1
        for t5 in range(NQC):
            xp = t5 % 2
            sch.op("sync", lambda e, t5=t5, p=xp: e.dma_start(
                out=xth[p], in_=x2_h[t5]),
                w=[("tmm", x2_free[xp])], i=(f"lx2{xp}", 16))
            sch.op("sync", lambda e, t5=t5, p=xp: e.dma_start(
                out=xtl[p], in_=x2_l[t5]),
                i=(f"lx2{xp}", 16))
            lx2v = sch.val(f"lx2{xp}")
            if it > 0:
                if t5 == 0:
                    w2_ready[1] = w2_load(it, 1)
                elif t5 == 2 and it + 1 < NIT:
                    w1_load_half(it + 1, (it + 1) % 2, 0)
                elif t5 == 4 and it + 1 < NIT:
                    w1_ready[(it + 1) % 2] = w1_load_half(it + 1, (it + 1) % 2, 1)
            sch.op("sync", lambda e, t5=t5: e.dma_start(
                out=x1t.rearrange("p (c d) -> p c d", c=4),
                in_=x1s.rearrange("(b p) d -> p b d", p=128)[:, t5 * 4:(t5 + 1) * 4, :]),
                w=[("sx1c", x1_st), ("vec", max(y_guard))], i=("lx1c", 16))
            lx1v = sch.val("lx1c")
            if it + 2 < NIT:
                pump_jobs(2)

            for fc in range(8):
                hp = fc % 2
                hmm = None
                for hc in range(HC):
                    st_h = w1h[wp][:, hc * 1024 + fc * 128: hc * 1024 + (fc + 1) * 128]
                    st_l = w1l[wp][:, hc * 1024 + fc * 128: hc * 1024 + (fc + 1) * 128]
                    mv_h = xth[xp][:, hc * 512:(hc + 1) * 512]
                    mv_l = xtl[xp][:, hc * 512:(hc + 1) * 512]
                    w_ = []
                    if hc == 0:
                        w_ = [("vec", h_guard[hp])]
                        if fc == 0:
                            w_ += [(f"lw{wp}", w1_ready[wp]), (f"lx2{xp}", lx2v),
                                   ("tmm", h_cons)]
                    sch.op("tensor", lambda e, a=st_h, b=mv_h, p=hp, f=(hc == 0): e.matmul(
                        ps_f[p][:], a, b, start=f, stop=False), w=w_)
                    sch.op("tensor", lambda e, a=st_h, b=mv_l, p=hp: e.matmul(
                        ps_f[p][:], a, b, start=False, stop=False))
                    hmm = sch.op("tensor", lambda e, a=st_l, b=mv_h, p=hp,
                                 l=(hc == HC - 1): e.matmul(
                        ps_f[p][:], a, b, start=False, stop=l),
                        i=("tmm", 1) if hc == HC - 1 else None)
                bias_col = b1T[:, e_i * 32 + fq * 8 + fc: e_i * 32 + fq * 8 + fc + 1]
                sv = sch.op("scalar", lambda e, p=hp, b=bias_col: e.activation(
                    h32, ps_f[p][:], AF.Gelu, bias=b),
                    w=[("tmm", hmm), ("vec", sch.val("vec"))], i=("sca", 1))
                sh = sch.op("scalar", lambda e, fc=fc: e.copy(
                    hh[:, fc * 512:(fc + 1) * 512], h32), w=[("sca", sv)], i=("sca", 1))
                vv = sch.op("vector", lambda e, fc=fc: e.tensor_sub(
                    hl[:, fc * 512:(fc + 1) * 512], h32, hh[:, fc * 512:(fc + 1) * 512]),
                    w=[("sca", sh)], i=("vec", 1))
                h_guard[hp] = vv
                h_rdy_v[fc] = vv
                h_rdy_s[fc] = sh

            for jh in range(2):
                for tb in range(4):
                    ymm = None
                    first_grp = (jh == 0 and tb == 0)
                    for fc in range(8):
                        a_h = hh[:, fc * 512 + tb * 128: fc * 512 + (tb + 1) * 128]
                        a_l = hl[:, fc * 512 + tb * 128: fc * 512 + (tb + 1) * 128]
                        m_h = w2h[jh][:, fc * 512:(fc + 1) * 512]
                        m_l = w2l[jh][:, fc * 512:(fc + 1) * 512]
                        w_ = []
                        if fc == 0:
                            w_ = [("vec", y_guard[tb]), (f"lw2{jh}", w2_ready[jh])]
                            if first_grp:
                                w_ += [("vec", h_rdy_v[6]), ("sca", h_rdy_s[6])]
                        elif fc == 7 and first_grp:
                            w_ = [("vec", h_rdy_v[7]), ("sca", h_rdy_s[7])]
                        sch.op("tensor", lambda e, a=a_h, b=m_h, tb=tb,
                               f=(fc == 0): e.matmul(
                            ybank[tb], a, b, start=f, stop=False), w=w_)
                        sch.op("tensor", lambda e, a=a_h, b=m_l, tb=tb: e.matmul(
                            ybank[tb], a, b, start=False, stop=False))
                        ymm = sch.op("tensor", lambda e, a=a_l, b=m_h, tb=tb,
                                     l=(fc == 7): e.matmul(
                            ybank[tb], a, b, start=False, stop=l),
                            i=("tmm", 1) if fc == 7 else None)
                    tt_g = t5 * 4 + tb
                    csc = comb[:, tt_g * 9 + e_i: tt_g * 9 + e_i + 1] if e_i < 8 else 1.0
                    dslc = x1t[:, tb * 1024 + jh * 512: tb * 1024 + (jh + 1) * 512]
                    vv = sch.op("vector", lambda e, tb=tb, c=csc, d=dslc: e.scalar_tensor_tensor(
                        d, ybank[tb], c, d, ALU.mult, ALU.add),
                        w=[("tmm", ymm), ("lx1c", lx1v)], i=("vec", 1))
                    y_guard[tb] = vv
                if jh == 0 and t5 == NQC - 1 and not last_it:
                    w2_free[0] = sch.val("tmm")
                    w2_ready[0] = w2_load(it + 1, 0)
            h_cons = sch.val("tmm")
            dst = out if last_it else x1s
            sch.op("scalar", lambda e, t5=t5, d=dst: e.dma_start(
                out=d.rearrange("(b p) d -> p b d", p=128)[:, t5 * 4:(t5 + 1) * 4, :],
                in_=x1t.rearrange("p (c d) -> p c d", c=4)),
                w=[("vec", sch.val("vec"))], i=("sx1c", 16))
            x1_st = sch.val("sx1c")
            x2_free[xp] = h_cons
        w1_free[wp] = sch.val("tmm")
        w2_free[1] = sch.val("tmm")

    sch.wait("sync", ("sx1c", sch.val("sx1c")))
    tgt = sch.cnt["bar"] + 1
    sch.op("sync", lambda e: e.nop(), i=("bar", 1))
    sch.wait("gpsimd", ("bar", tgt))

    blk = es.enter_context(nc.Block())

    @blk.tensor
    def _(e):
        for f in sch.q["tensor"]:
            f(e)

    @blk.vector
    def _(e):
        for f in sch.q["vector"]:
            f(e)

    @blk.scalar
    def _(e):
        for f in sch.q["scalar"]:
            f(e)

    @blk.gpsimd
    def _(e):
        for f in sch.q["gpsimd"]:
            f(e)

    @blk.sync
    def _(e):
        for f in sch.q["sync"]:
            f(e)

    es.close()
    return nc


def kernel(L0_attn_X, L0_attn_W_Q, L0_attn_W_K, L0_attn_W_V, L0_ffn_gate_w,
           L0_ffn_W1, L0_ffn_b1, L0_ffn_W2, L0_ffn_b2, L0_ffn_sW1, L0_ffn_sb1,
           L0_ffn_sW2, L0_ffn_sb2):
    nc = build_nc(debug=DBG)
    shared = {
        "wq": np.ascontiguousarray(L0_attn_W_Q, dtype=np.float32),
        "wk": np.ascontiguousarray(L0_attn_W_K, dtype=np.float32),
        "wv": np.ascontiguousarray(L0_attn_W_V, dtype=np.float32),
        "gw": np.ascontiguousarray(L0_ffn_gate_w, dtype=np.float32),
        "w1": np.ascontiguousarray(L0_ffn_W1, dtype=np.float32),
        "b1v": np.ascontiguousarray(L0_ffn_b1, dtype=np.float32),
        "w2": np.ascontiguousarray(L0_ffn_W2, dtype=np.float32),
        "b2v": np.ascontiguousarray(L0_ffn_b2, dtype=np.float32),
        "sw1": np.ascontiguousarray(L0_ffn_sW1, dtype=np.float32),
        "sb1v": np.ascontiguousarray(L0_ffn_sb1, dtype=np.float32),
        "sw2": np.ascontiguousarray(L0_ffn_sW2, dtype=np.float32),
        "sb2v": np.ascontiguousarray(L0_ffn_sb2, dtype=np.float32),
    }
    in_maps = []
    for c in range(8):
        m = dict(shared)
        m["x_in"] = np.ascontiguousarray(L0_attn_X[c], dtype=np.float32)
        in_maps.append(m)
    res = run_bass_kernel_spmd(nc, in_maps, core_ids=list(range(8)))
    return np.stack([np.asarray(r["out"]) for r in res.results], axis=0)


# version 13
# speedup vs baseline: 1.2406x; 1.2406x over previous
import numpy as np
from contextlib import ExitStack

import concourse.bass as bass
import concourse.mybir as mybir
from concourse.bass_utils import run_bass_kernel_spmd

F32 = mybir.dt.float32
BF16 = mybir.dt.bfloat16
AF = mybir.ActivationFunctionType
ALU = mybir.AluOpType
AX = mybir.AxisListType

T = 4096
D = 1024
FF = 4096
HC = 8
NE = 9
NQC = 8
NTT = 32
EPS = 1e-5
SCL = 1.0 / 32.0

DBG = False


class Sched:
    ENGS = ("tensor", "vector", "scalar", "gpsimd", "sync")

    def __init__(self, nc, es):
        self.nc = nc
        self.q = {e: [] for e in self.ENGS}
        self.cnt = {}
        self.h = {}
        self.es = es

    def sem(self, *names):
        for n in names:
            self.h[n] = self.es.enter_context(self.nc.semaphore(n))
            self.cnt[n] = 0

    def val(self, name):
        return self.cnt[name]

    def op(self, eng, fn, w=(), i=None):
        waits = [(self.h[s], v) for s, v in w if v > 0]
        tgt = None
        if i is not None:
            s, n = i
            self.cnt[s] += n
            tgt = (self.h[s], n)

        def emit(e, waits=waits, fn=fn, tgt=tgt):
            for hh, v in waits:
                e.wait_ge(hh, v)
            ins = fn(e)
            if tgt is not None:
                ins.then_inc(*tgt)
        self.q[eng].append(emit)
        return self.cnt[i[0]] if i is not None else None

    def wait(self, eng, *w):
        waits = [(self.h[s], v) for s, v in w if v > 0]

        def emit(e, waits=waits):
            for hh, v in waits:
                e.wait_ge(hh, v)
        self.q[eng].append(emit)

    def barrier(self, extra_sync_waits=()):
        tgt = self.cnt["bar"] + 5
        for eng in self.ENGS:
            if eng == "sync":
                self.wait("sync", *extra_sync_waits)
            self.op(eng, lambda e: e.nop(), i=("bar", 1))
        for eng in self.ENGS:
            self.wait(eng, ("bar", tgt))


def build_nc(debug=False):
    nc = bass.Bass("TRN2", target_bir_lowering=False)
    es = ExitStack()

    xin = nc.dram_tensor("x_in", [T, D], F32, kind="ExternalInput").ap()
    wq = nc.dram_tensor("wq", [D, D], F32, kind="ExternalInput").ap()
    wk = nc.dram_tensor("wk", [D, D], F32, kind="ExternalInput").ap()
    wv = nc.dram_tensor("wv", [D, D], F32, kind="ExternalInput").ap()
    gw = nc.dram_tensor("gw", [8, D], F32, kind="ExternalInput").ap()
    w1 = nc.dram_tensor("w1", [8, D, FF], F32, kind="ExternalInput").ap()
    b1v = nc.dram_tensor("b1v", [8, FF], F32, kind="ExternalInput").ap()
    w2 = nc.dram_tensor("w2", [8, FF, D], F32, kind="ExternalInput").ap()
    b2v = nc.dram_tensor("b2v", [8, D], F32, kind="ExternalInput").ap()
    sw1 = nc.dram_tensor("sw1", [D, FF], F32, kind="ExternalInput").ap()
    sb1v = nc.dram_tensor("sb1v", [FF], F32, kind="ExternalInput").ap()
    sw2 = nc.dram_tensor("sw2", [FF, D], F32, kind="ExternalInput").ap()
    sb2v = nc.dram_tensor("sb2v", [D], F32, kind="ExternalInput").ap()
    out = nc.dram_tensor("out", [T, D], F32, kind="ExternalOutput").ap()

    sk = "ExternalOutput" if debug else "Internal"
    qs_h = nc.dram_tensor("qs_h", [NTT, 128, HC * 128], BF16, kind=sk).ap()
    qs_l = nc.dram_tensor("qs_l", [NTT, 128, HC * 128], BF16, kind=sk).ap()
    ks_h = nc.dram_tensor("ks_h", [NQC, 128, HC * 512], BF16, kind=sk).ap()
    ks_l = nc.dram_tensor("ks_l", [NQC, 128, HC * 512], BF16, kind=sk).ap()
    vts_h = nc.dram_tensor("vts_h", [NTT, 128, D], BF16, kind=sk).ap()
    vts_l = nc.dram_tensor("vts_l", [NTT, 128, D], BF16, kind=sk).ap()
    x2_h = nc.dram_tensor("x2_h", [NQC, 128, HC * 512], BF16, kind=sk).ap()
    x2_l = nc.dram_tensor("x2_l", [NQC, 128, HC * 512], BF16, kind=sk).ap()
    x1s = nc.dram_tensor("x1s", [T, D], F32, kind=sk).ap()
    cmbs = nc.dram_tensor("cmbs", [128, NTT * 9], F32, kind=sk).ap() if debug else None
    w1s_h = nc.dram_tensor("w1s_h", [NE, 4, 128, HC * 1024], BF16, kind="Internal").ap()
    w1s_l = nc.dram_tensor("w1s_l", [NE, 4, 128, HC * 1024], BF16, kind="Internal").ap()
    w2s_h = nc.dram_tensor("w2s_h", [NE, 4, 2, 128, 8 * 512], BF16, kind="Internal").ap()
    w2s_l = nc.dram_tensor("w2s_l", [NE, 4, 2, 128, 8 * 512], BF16, kind="Internal").ap()

    sch = Sched(nc, es)
    sch.sem("tmm", "vec", "sca", "gpc", "bar")
    sch.sem("lx0", "lx1", "lst0", "lst1", "lq0", "lq1", "lk0", "lk1",
            "lvt0", "lvt1", "lvt2", "lvt3", "lx20", "lx21", "lx1c", "lw0", "lw1", "lw20",
            "lw21", "lsetup")
    sch.sem("sev", "sxlt", "sx1b0", "sx1b1", "ssw0", "ssw1", "sx1c")

    A = es.enter_context

    BP = A(nc.sbuf_tensor("BP", [128, 78848], BF16))
    FP = A(nc.sbuf_tensor("FP", [128, 8192], F32))

    def bsl(a, b):
        return BP[:, a:b]

    def fsl(a, b):
        return FP[:, a:b]

    identb = A(nc.sbuf_tensor("identb", [128, 128], BF16))
    identf = A(nc.sbuf_tensor("identf", [128, 128], F32))
    gwT_h = A(nc.sbuf_tensor("gwT_h", [128, HC * 8], BF16))
    gwT_l = A(nc.sbuf_tensor("gwT_l", [128, HC * 8], BF16))
    b1T = A(nc.sbuf_tensor("b1T", [128, NE * 32], F32))
    b2e = A(nc.sbuf_tensor("b2e", [9, D], F32))
    comb = A(nc.sbuf_tensor("comb", [128, NTT * 9], F32))
    stat = A(nc.sbuf_tensor("stat", [128, 64], F32))
    cx = A(nc.sbuf_tensor("cx", [128, 48], F32))
    cts = A(nc.sbuf_tensor("cts", [16, 128], F32))
    bpre = A(nc.sbuf_tensor("bpre", [32, 128], F32))
    gwpre = A(nc.sbuf_tensor("gwpre", [8, D], F32))

    wqkv = {
        "q": (bsl(0, 8192), bsl(8192, 16384)),
        "k": (bsl(16384, 24576), bsl(24576, 32768)),
        "v": (bsl(32768, 40960), bsl(40960, 49152)),
    }
    xh = [bsl(49152, 50176), bsl(50176, 51200)]
    xl = [bsl(51200, 52224), bsl(52224, 53248)]
    xlt_h = bsl(53248, 57344)
    xlt_l = bsl(57344, 61440)
    swh_a = [bsl(61440, 62464), bsl(62464, 63488)]
    swl_a = [bsl(63488, 64512), bsl(64512, 65536)]
    swh_c = [bsl(73728, 74752), bsl(74752, 75776)]
    swl_c = [bsl(75776, 76800), bsl(76800, 77824)]
    jnk = bsl(69632, 70656)
    ev_h = bsl(70656, 74752)
    ev_l = bsl(74752, 78848)
    stg_a = [fsl(0, 1024), fsl(1024, 2048)]
    stg_c = [fsl(5120, 6144), fsl(6144, 7168)]
    xt = [fsl(4096, 5120), fsl(5120, 6144)]
    xc = [fsl(6144, 7168), fsl(7168, 8192)]

    kk_h = [bsl(0, 4096), bsl(8192, 12288)]
    kk_l = [bsl(4096, 8192), bsl(12288, 16384)]
    qq_h = [bsl(16384, 17408), bsl(18432, 19456)]
    qq_l = [bsl(17408, 18432), bsl(19456, 20480)]
    aa_h = bsl(20480, 24576)
    aa_l = bsl(24576, 28672)
    vt_h = [bsl(28672, 29696), bsl(30720, 31744)]
    vt_l = [bsl(29696, 30720), bsl(31744, 32768)]
    at_sb = [bsl(32768, 33024), bsl(33024, 33280)]
    ss = fsl(0, 4096)

    w1h = [bsl(0, 8192), bsl(16384, 24576)]
    w1l = [bsl(8192, 16384), bsl(24576, 32768)]
    w2h = [bsl(32768, 36864), bsl(40960, 45056)]
    w2l = [bsl(36864, 40960), bsl(45056, 49152)]
    xth = [bsl(49152, 53248), bsl(57344, 61440)]
    xtl = [bsl(53248, 57344), bsl(61440, 65536)]
    hh = bsl(65536, 69632)
    hl = bsl(69632, 73728)
    x1t = fsl(0, 4096)
    h32 = fsl(4096, 4608)

    ps_f = [A(nc.psum_tensor(f"ps_f{i}", [128, 512], F32)) for i in range(4)]
    ps_w = A(nc.psum_tensor("ps_w", [128, 1024], F32))
    ps_b = [A(nc.psum_tensor(f"ps_b{i}", [128, 1024], BF16)) for i in range(2)]

    g1 = sch.op("gpsimd", lambda g: g.memset(identb[:], 0.0), i=("gpc", 1))
    sch.op("gpsimd", lambda g: g.affine_select(
        out=identb[:], in_=identb[:], compare_op=ALU.not_equal, fill=1.0,
        base=0, pattern=[[-1, 128]], channel_multiplier=1),
        w=[("gpc", g1)], i=("gpc", 1))
    g3 = sch.op("gpsimd", lambda g: g.memset(identf[:], 0.0), i=("gpc", 1))
    sch.op("gpsimd", lambda g: g.affine_select(
        out=identf[:], in_=identf[:], compare_op=ALU.not_equal, fill=1.0,
        base=0, pattern=[[-1, 128]], channel_multiplier=1),
        w=[("gpc", g3)], i=("gpc", 1))
    sch.op("gpsimd", lambda g: g.memset(
        comb[:].rearrange("p (t e) -> p t e", e=9)[:, :, 8:9], 1.0), i=("gpc", 1))
    gp_setup = sch.val("gpc")

    sch.op("sync", lambda e: e.dma_start(out=b2e[0:8, :], in_=b2v), i=("lsetup", 16))
    sch.op("sync", lambda e: e.dma_start(out=b2e[8:9, :], in_=sb2v[None, :]), i=("lsetup", 16))
    sch.op("sync", lambda e: e.dma_start(out=gwpre[:], in_=gw), i=("lsetup", 16))
    gw_ld = sch.val("lsetup")

    for hc in range(HC):
        tv = sch.op("tensor", lambda e, hc=hc: e.transpose(
            ps_f[0][:, 0:8], gwpre[:, hc * 128:(hc + 1) * 128], identf[0:8, 0:8]),
            w=[("lsetup", gw_ld), ("gpc", gp_setup),
               ("sca", sch.val("sca")), ("vec", sch.val("vec"))], i=("tmm", 1))
        sv = sch.op("scalar", lambda e, hc=hc: e.copy(
            gwT_h[:, hc * 8:(hc + 1) * 8], ps_f[0][:, 0:8]),
            w=[("tmm", tv)], i=("sca", 1))
        sch.op("vector", lambda e, hc=hc: e.tensor_sub(
            gwT_l[:, hc * 8:(hc + 1) * 8], ps_f[0][:, 0:8],
            gwT_h[:, hc * 8:(hc + 1) * 8]),
            w=[("sca", sv)], i=("vec", 1))

    for e_i in range(NE):
        src = b1v[e_i] if e_i < 8 else sb1v
        ld = sch.op("sync", lambda e, src=src: e.dma_start(
            out=bpre[:], in_=src.rearrange("(fc p) -> fc p", p=128)),
            w=[("tmm", sch.val("tmm"))], i=("lsetup", 16))
        tv = sch.op("tensor", lambda e: e.transpose(
            ps_f[1][:, 0:32], bpre[:], identf[0:32, 0:32]),
            w=[("lsetup", ld), ("vec", sch.val("vec"))], i=("tmm", 1))
        sch.op("vector", lambda e, e_i=e_i: e.tensor_copy(
            b1T[:, e_i * 32:(e_i + 1) * 32], ps_f[1][:, 0:32]),
            w=[("tmm", tv)], i=("vec", 1))

    _spl = {"last_vec": [0, 0], "last_store": [0, 0], "par": 0, "stg": None,
            "swh": None, "swl": None}

    def split_chunk(src_ap, ncols, dst_hi, dst_lo, store=None):
        par = _spl["par"]
        _spl["par"] ^= 1
        stg = _spl["stg"]
        lsem, ssem = f"lst{par}", f"ssw{par}"
        sch.op("sync", lambda e, s=src_ap, g=stg[par], n=ncols: e.dma_start(
            out=g[:, :n], in_=s),
            w=[("gpc", _spl["last_vec"][par])], i=(lsem, 16))
        lv = sch.val(lsem)
        if store is None:
            hi_t, lo_t = dst_hi, dst_lo
            gw_ = [(lsem, lv)]
        else:
            hi_t, lo_t = _spl["swh"][par][:, :ncols], _spl["swl"][par][:, :ncols]
            gw_ = [(lsem, lv), (ssem, _spl["last_store"][par])]
        gv = sch.op("gpsimd", lambda e, g=stg[par], n=ncols, h=hi_t: e.tensor_copy(
            h, g[:, :n]), w=gw_, i=("gpc", 1))
        vv = sch.op("gpsimd", lambda e, g=stg[par], n=ncols, h=hi_t, l=lo_t: e.tensor_sub(
            l, g[:, :n], h), w=[("gpc", gv)], i=("gpc", 1))
        _spl["last_vec"][par] = vv
        if store is not None:
            dh, dl = store
            if isinstance(dh, tuple):
                for jh in range(2):
                    sch.op("gpsimd", lambda e, d=dh[jh], h=hi_t, jh=jh: e.dma_start(
                        out=d, in_=h[:, jh * 512:(jh + 1) * 512]),
                        w=[("gpc", gv)], i=(ssem, 16))
                    sch.op("gpsimd", lambda e, d=dl[jh], l=lo_t, jh=jh: e.dma_start(
                        out=d, in_=l[:, jh * 512:(jh + 1) * 512]),
                        w=[("gpc", vv)], i=(ssem, 16))
            else:
                sch.op("gpsimd", lambda e, n=ncols, d=dh, h=hi_t: e.dma_start(
                    out=d, in_=h[:, :n]), w=[("gpc", gv)], i=(ssem, 16))
                sch.op("gpsimd", lambda e, n=ncols, d=dl, l=lo_t: e.dma_start(
                    out=d, in_=l[:, :n]), w=[("gpc", vv)], i=(ssem, 16))
            _spl["last_store"][par] = sch.val(ssem)

    _spl["stg"] = stg_a
    _spl["swh"], _spl["swl"] = swh_a, swl_a
    for nm, wsrc in (("q", wq), ("k", wk), ("v", wv)):
        hi_b, lo_b = wqkv[nm]
        for hc in range(HC):
            split_chunk(wsrc[hc * 128:(hc + 1) * 128, :], 1024,
                        hi_b[:, hc * 1024:(hc + 1) * 1024],
                        lo_b[:, hc * 1024:(hc + 1) * 1024])
    wqkv_done_gpc = sch.val("gpc")

    NITC = NE * 4
    split_jobs_by_it = [[] for _ in range(NITC)]
    for it_ in range(NITC):
        e_i, fq = divmod(it_, 4)
        w1src = w1[e_i] if e_i < 8 else sw1
        w2src = w2[e_i] if e_i < 8 else sw2
        for hc in range(HC):
            split_jobs_by_it[it_].append((
                w1src[hc * 128:(hc + 1) * 128, fq * 1024:(fq + 1) * 1024],
                (w1s_h[e_i, fq, :, hc * 1024:(hc + 1) * 1024],
                 w1s_l[e_i, fq, :, hc * 1024:(hc + 1) * 1024])))
        for fc in range(fq * 8, fq * 8 + 8):
            split_jobs_by_it[it_].append((
                w2src[fc * 128:(fc + 1) * 128, :],
                ((w2s_h[e_i, fq, 0, :, (fc % 8) * 512:(fc % 8) * 512 + 512],
                  w2s_h[e_i, fq, 1, :, (fc % 8) * 512:(fc % 8) * 512 + 512]),
                 (w2s_l[e_i, fq, 0, :, (fc % 8) * 512:(fc % 8) * 512 + 512],
                  w2s_l[e_i, fq, 1, :, (fc % 8) * 512:(fc % 8) * 512 + 512]))))
    split_marks = [None] * NITC
    split_flat = [(it_, src, store) for it_ in range(NITC)
                  for src, store in split_jobs_by_it[it_]]
    split_pos = [0]

    def pump_jobs(n):
        for _ in range(n):
            if split_pos[0] >= len(split_flat):
                return
            it_, src, store = split_flat[split_pos[0]]
            split_chunk(src, 1024, None, None, store=store)
            split_pos[0] += 1
            if (split_pos[0] >= len(split_flat)
                    or split_flat[split_pos[0]][0] != it_):
                split_marks[it_] = (sch.val("ssw0"), sch.val("ssw1"))

    def ensure_pumped(it_):
        while it_ < NITC and split_marks[it_] is None:
            pump_jobs(1)

    ln_prev_s3c = [0, 0]
    xhl_free_tmm = [0, 0]

    def emit_ln(par, xtw=0):
        base = 24 * par
        s1, mn, ss_, va, sd, r0, t_, rs = (stat[:, base + i:base + i + 1] for i in range(8))
        xtp, xcp, xhp, xlp = xt[par], xc[par], xh[par], xl[par]
        v1 = sch.op("vector", lambda e: e.reduce_sum(s1, xtp, axis=AX.X),
                    w=[("sca", ln_prev_s3c[par]), ("vec", xtw)], i=("vec", 1))
        v2 = sch.op("vector", lambda e: e.tensor_scalar_mul(mn, s1, 1.0 / D),
                    w=[("vec", v1)], i=("vec", 1))
        v3 = sch.op("vector", lambda e: e.tensor_scalar_sub(xcp, xtp, mn),
                    w=[("vec", v2)], i=("vec", 1))
        s1c = sch.op("scalar", lambda e: e.activation(
            jnk, xcp, AF.Square, accum_out=ss_), w=[("vec", v3)], i=("sca", 1))
        v4 = sch.op("vector", lambda e: e.tensor_scalar(
            va, ss_, 1.0 / D, EPS, ALU.mult, ALU.add), w=[("sca", s1c)], i=("vec", 1))
        s2c = sch.op("scalar", lambda e: e.activation(sd, va, AF.Sqrt),
                     w=[("vec", v4)], i=("sca", 1))
        n1 = sch.op("vector", lambda e: e.reciprocal(r0, sd), w=[("sca", s2c)], i=("vec", 1))
        n2 = sch.op("vector", lambda e: e.tensor_mul(t_, r0, r0), w=[("vec", n1)], i=("vec", 1))
        n3 = sch.op("vector", lambda e: e.tensor_mul(t_, t_, va), w=[("vec", n2)], i=("vec", 1))
        n4 = sch.op("vector", lambda e: e.tensor_scalar(
            t_, t_, -0.5, 1.5, ALU.mult, ALU.add), w=[("vec", n3)], i=("vec", 1))
        v6 = sch.op("vector", lambda e: e.tensor_mul(rs, r0, t_), w=[("vec", n4)], i=("vec", 1))
        s3c = sch.op("scalar", lambda e: e.activation(
            xhp, xcp, AF.Identity, scale=rs),
            w=[("vec", v6), ("tmm", xhl_free_tmm[par])], i=("sca", 1))
        v7 = sch.op("vector", lambda e: e.tensor_scalar_mul(xcp, xcp, rs),
                    w=[("vec", v6), ("sca", s3c)], i=("vec", 1))
        v8 = sch.op("vector", lambda e: e.tensor_sub(xlp, xcp, xhp),
                    w=[("sca", s3c), ("vec", v7), ("tmm", xhl_free_tmm[par])],
                    i=("vec", 1))
        ln_prev_s3c[par] = s3c
        return v8, s3c

    trp_evac = [0, 0]

    def emit_tr_to_fm(tile_in_chunk, par, vdone, sdone, xlt_war=0):
        tp = tile_in_chunk % 2
        last_t = None
        for hc in range(HC):
            w_ = [("sca", sdone), ("vec", trp_evac[tp])] if hc == 0 else []
            last_t = sch.op("tensor", lambda e, hc=hc, tp=tp, p=par: e.transpose(
                ps_b[tp][:, hc * 128:(hc + 1) * 128],
                xh[p][:, hc * 128:(hc + 1) * 128], identb[:]), w=w_, i=("tmm", 1))
        vh = sch.op("vector", lambda e, tp=tp, ti=tile_in_chunk: e.tensor_copy(
            xlt_h.rearrange("p (hc t) -> p hc t", hc=HC)[:, :, ti * 128:(ti + 1) * 128],
            ps_b[tp][:].rearrange("p (hc t) -> p hc t", hc=HC)),
            w=[("tmm", max(last_t, xlt_war))], i=("vec", 1))
        for hc in range(HC):
            w_ = [("vec", max(vh, vdone))] if hc == 0 else []
            last_t = sch.op("tensor", lambda e, hc=hc, tp=tp, p=par: e.transpose(
                ps_b[tp][:, hc * 128:(hc + 1) * 128],
                xl[p][:, hc * 128:(hc + 1) * 128], identb[:]), w=w_, i=("tmm", 1))
        vl = sch.op("vector", lambda e, tp=tp, ti=tile_in_chunk: e.tensor_copy(
            xlt_l.rearrange("p (hc t) -> p hc t", hc=HC)[:, :, ti * 128:(ti + 1) * 128],
            ps_b[tp][:].rearrange("p (hc t) -> p hc t", hc=HC)),
            w=[("tmm", max(last_t, xlt_war))], i=("vec", 1))
        trp_evac[tp] = vl
        xhl_free_tmm[par] = last_t
        return vl

    qk_guard = [(0, 0), (0, 0)]
    v_guard = [(0, 0), (0, 0)]
    ev_store_guard = 0
    xt_free_vec = [0, 0]
    xlt_war_tmm = 0

    for qc in range(NQC):
        for tt in range(4):
            p_ = tt % 2
            g = qc * 4 + tt
            ld = sch.op("sync", lambda e, g=g, p=p_: e.dma_start(
                out=xt[p], in_=xin[g * 128:(g + 1) * 128, :]),
                w=[("vec", xt_free_vec[p_])], i=(f"lx{p_}", 16))
            sch.wait("vector", (f"lx{p_}", ld))
            if qc == 0 and tt == 0:
                sch.wait("tensor", ("gpc", wqkv_done_gpc))
            vdone, sdone = emit_ln(p_)
            xt_free_vec[p_] = vdone
            emit_tr_to_fm(tt, p_, vdone, sdone, xlt_war=xlt_war_tmm)
            pump_jobs(2)
        xlt_ready = sch.val("vec")

        for nm, (dst_h, dst_l) in (("q", (qs_h, qs_l)), ("k", (ks_h, ks_l))):
            whi, wlo = wqkv[nm]
            for oc in range(HC):
                p_ = oc % 2
                sg, vg = qk_guard[p_]
                mmv = None
                for hc in range(HC):
                    st_h = whi[:, hc * 1024 + oc * 128: hc * 1024 + (oc + 1) * 128]
                    st_l = wlo[:, hc * 1024 + oc * 128: hc * 1024 + (oc + 1) * 128]
                    mv_h = xlt_h[:, hc * 512:(hc + 1) * 512]
                    mv_l = xlt_l[:, hc * 512:(hc + 1) * 512]
                    w_ = [("vec", max(xlt_ready, vg)), ("sca", sg)] if hc == 0 else []
                    sch.op("tensor", lambda e, a=st_h, b=mv_h, p=p_, f=(hc == 0): e.matmul(
                        ps_f[p][:], a, b, start=f, stop=False), w=w_)
                    sch.op("tensor", lambda e, a=st_h, b=mv_l, p=p_: e.matmul(
                        ps_f[p][:], a, b, start=False, stop=False))
                    mmv = sch.op("tensor", lambda e, a=st_l, b=mv_h, p=p_,
                                 l=(hc == HC - 1): e.matmul(
                        ps_f[p][:], a, b, start=False, stop=l),
                        i=("tmm", 1) if hc == HC - 1 else None)
                sv = sch.op("scalar", lambda e, oc=oc, p=p_: e.copy(
                    ev_h[:, oc * 512:(oc + 1) * 512], ps_f[p][:]),
                    w=[("tmm", mmv), ("sev", ev_store_guard)], i=("sca", 1))
                vv = sch.op("vector", lambda e, oc=oc, p=p_: e.tensor_sub(
                    ev_l[:, oc * 512:(oc + 1) * 512], ps_f[p][:],
                    ev_h[:, oc * 512:(oc + 1) * 512]),
                    w=[("sca", sv), ("sev", ev_store_guard)], i=("vec", 1))
                qk_guard[p_] = (sv, vv)
            if nm == "q":
                oap_h = dst_h[qc * 4:(qc + 1) * 4].rearrange(
                    "q p (hc t) -> p hc q t", t=128)
                oap_l = dst_l[qc * 4:(qc + 1) * 4].rearrange(
                    "q p (hc t) -> p hc q t", t=128)
                iap_h = ev_h.rearrange("p (hc q t) -> p hc q t", q=4, t=128)
                iap_l = ev_l.rearrange("p (hc q t) -> p hc q t", q=4, t=128)
            else:
                oap_h = dst_h[qc]
                oap_l = dst_l[qc]
                iap_h = ev_h
                iap_l = ev_l
            sch.op("scalar", lambda e, d=oap_h, s=iap_h: e.dma_start(
                out=d, in_=s),
                w=[("sca", max(qk_guard[0][0], qk_guard[1][0]))], i=("sev", 16))
            sch.op("scalar", lambda e, d=oap_l, s=iap_l: e.dma_start(
                out=d, in_=s),
                w=[("vec", max(qk_guard[0][1], qk_guard[1][1]))], i=("sev", 16))
            ev_store_guard = sch.val("sev")

        whi, wlo = wqkv["v"]
        for tt in range(4):
            for jh in range(2):
                p_ = (tt * 2 + jh) % 2
                sg, vg = v_guard[p_]
                mmv = None
                for hc in range(HC):
                    st_h = xlt_h[:, hc * 512 + tt * 128: hc * 512 + (tt + 1) * 128]
                    st_l = xlt_l[:, hc * 512 + tt * 128: hc * 512 + (tt + 1) * 128]
                    mv_h = whi[:, hc * 1024 + jh * 512: hc * 1024 + (jh + 1) * 512]
                    mv_l = wlo[:, hc * 1024 + jh * 512: hc * 1024 + (jh + 1) * 512]
                    w_ = [("sca", sg), ("vec", vg)] if hc == 0 else []
                    sch.op("tensor", lambda e, a=st_h, b=mv_h, p=p_, f=(hc == 0): e.matmul(
                        ps_f[2 + p][:], a, b, start=f, stop=False), w=w_)
                    sch.op("tensor", lambda e, a=st_h, b=mv_l, p=p_: e.matmul(
                        ps_f[2 + p][:], a, b, start=False, stop=False))
                    mmv = sch.op("tensor", lambda e, a=st_l, b=mv_h, p=p_,
                                 l=(hc == HC - 1): e.matmul(
                        ps_f[2 + p][:], a, b, start=False, stop=l),
                        i=("tmm", 1) if hc == HC - 1 else None)
                sv = sch.op("scalar", lambda e, tt=tt, jh=jh, p=p_: e.copy(
                    ev_h[:, tt * 1024 + jh * 512: tt * 1024 + (jh + 1) * 512],
                    ps_f[2 + p][:]),
                    w=[("tmm", mmv), ("sev", ev_store_guard)], i=("sca", 1))
                vv = sch.op("vector", lambda e, tt=tt, jh=jh, p=p_: e.tensor_sub(
                    ev_l[:, tt * 1024 + jh * 512: tt * 1024 + (jh + 1) * 512],
                    ps_f[2 + p][:],
                    ev_h[:, tt * 1024 + jh * 512: tt * 1024 + (jh + 1) * 512]),
                    w=[("sca", sv), ("sev", ev_store_guard)], i=("vec", 1))
                v_guard[p_] = (sv, vv)
        sch.op("scalar", lambda e, qc=qc: e.dma_start(
            out=vts_h[qc * 4:(qc + 1) * 4, :, :].rearrange("c p o -> p c o"),
            in_=ev_h.rearrange("p (c o) -> p c o", c=4)),
            w=[("sca", max(v_guard[0][0], v_guard[1][0]))], i=("sev", 16))
        sch.op("scalar", lambda e, qc=qc: e.dma_start(
            out=vts_l[qc * 4:(qc + 1) * 4, :, :].rearrange("c p o -> p c o"),
            in_=ev_l.rearrange("p (c o) -> p c o", c=4)),
            w=[("vec", max(v_guard[0][1], v_guard[1][1]))], i=("sev", 16))
        ev_store_guard = sch.val("sev")
        xlt_war_tmm = sch.val("tmm")

    ensure_pumped(1)

    sch.barrier(extra_sync_waits=[("sev", sch.val("sev")),
                                  ("ssw0", sch.val("ssw0")),
                                  ("ssw1", sch.val("ssw1"))])

    vtb_h = [bsl(70656 + s * 2048, 70656 + s * 2048 + 1024) for s in range(4)]
    vtb_l = [bsl(70656 + s * 2048 + 1024, 70656 + s * 2048 + 2048) for s in range(4)]

    kq_free = [0, 0]
    qq_free = [0, 0]
    vt_slot_free = [0, 0, 0, 0]
    s_guard = [0, 0]
    s_vec_guard = 0
    at_guard = [0, 0]
    av_guard = 0
    aa_guard_tmm = 0
    x1b_store = [0, 0]
    xlt_free = 0
    gate_stat_sca = 0
    ss_free_vec = 0
    trb_guard = [0, 0]

    nm_p = [stat[:, 36:37], stat[:, 40:41]]
    se_p = [stat[:, 37:38], stat[:, 41:42]]
    ra_p = [stat[:, 38:39], stat[:, 42:43]]
    pmax = stat[:, 48:56]

    def b_scores(qb):
        nonlocal s_vec_guard, ss_free_vec
        qpar = qb % 2
        sch.op("sync", lambda e, qb=qb, p=qpar: e.dma_start(
            out=qq_h[p], in_=qs_h[qb]),
            w=[("tmm", qq_free[qpar])], i=(f"lq{qpar}", 16))
        sch.op("sync", lambda e, qb=qb, p=qpar: e.dma_start(
            out=qq_l[p], in_=qs_l[qb]),
            i=(f"lq{qpar}", 16))
        lqv = sch.val(f"lq{qpar}")
        sch.op("sync", lambda e, qb=qb, p=qpar: e.dma_start(
            out=xt[p], in_=xin[qb * 128:(qb + 1) * 128, :]),
            w=[(f"sx1b{qpar}", x1b_store[qpar]), ("vec", xt_free_vec[qpar])],
            i=(f"lx{qpar}", 16))
        lxv = sch.val(f"lx{qpar}")

        pm_prev = 0
        for kb in range(8):
            kp = kb % 2
            sch.op("sync", lambda e, kb=kb, p=kp: e.dma_start(
                out=kk_h[p], in_=ks_h[kb]),
                w=[("tmm", kq_free[kp])], i=(f"lk{kp}", 16))
            sch.op("sync", lambda e, kb=kb, p=kp: e.dma_start(
                out=kk_l[p], in_=ks_l[kb]),
                i=(f"lk{kp}", 16))
            lkv = sch.val(f"lk{kp}")
            sp = kb % 2
            mmv = None
            for hc in range(HC):
                st_h = qq_h[qpar][:, hc * 128:(hc + 1) * 128]
                st_l = qq_l[qpar][:, hc * 128:(hc + 1) * 128]
                mv_h = kk_h[kp][:, hc * 512:(hc + 1) * 512]
                mv_l = kk_l[kp][:, hc * 512:(hc + 1) * 512]
                w_ = ([(f"lk{kp}", lkv), (f"lq{qpar}", lqv),
                       ("sca", s_guard[sp]), ("vec", s_vec_guard)] if hc == 0 else [])
                sch.op("tensor", lambda e, a=st_h, b=mv_h, p=sp, f=(hc == 0): e.matmul(
                    ps_f[p][:], a, b, start=f, stop=False), w=w_)
                sch.op("tensor", lambda e, a=st_h, b=mv_l, p=sp: e.matmul(
                    ps_f[p][:], a, b, start=False, stop=False))
                mmv = sch.op("tensor", lambda e, a=st_l, b=mv_h, p=sp,
                             l=(hc == HC - 1): e.matmul(
                    ps_f[p][:], a, b, start=False, stop=l),
                    i=("tmm", 1) if hc == HC - 1 else None)
            sv = sch.op("scalar", lambda e, kb=kb, p=sp: e.activation(
                ss[:, kb * 512:(kb + 1) * 512], ps_f[p][:], AF.Identity, scale=SCL),
                w=[("tmm", mmv), ("vec", ss_free_vec)], i=("sca", 1))
            pm = sch.op("vector", lambda e, kb=kb, p=sp: e.reduce_max(
                pmax[:, kb:kb + 1], ps_f[p][:], axis=AX.X),
                w=[("tmm", mmv), ("sca", sv), ("vec", pm_prev)], i=("vec", 1))
            pm_prev = pm
            s_guard[sp] = sv
            s_vec_guard = max(s_vec_guard, pm)
            kq_free[kp] = mmv
        qq_free[qpar] = sch.val("tmm")
        return {"qb": qb, "qpar": qpar, "qc": qb // 4, "tt": qb % 4,
                "lxv": lxv, "pm_last": pm_prev,
                "sv_last": (s_guard[0], s_guard[1])}

    def b_softmax(st):
        nm_, se_, ra_ = nm_p[st["qpar"]], se_p[st["qpar"]], ra_p[st["qpar"]]
        v1a = sch.op("vector", lambda e: e.reduce_max(nm_, pmax, axis=AX.X, negate=True),
                     w=[("vec", st["pm_last"]), ("tmm", aa_guard_tmm),
                        ("sca", gate_stat_sca)], i=("vec", 1))
        v1 = sch.op("vector", lambda e: e.tensor_scalar_mul(nm_, nm_, SCL),
                    w=[("vec", v1a)], i=("vec", 1))
        s1c = sch.op("scalar", lambda e: e.activation(
            aa_h, ss, AF.Exp, bias=nm_, accum_out=se_),
            w=[("vec", v1), ("sca", max(st["sv_last"]))], i=("sca", 1))
        s2c = sch.op("scalar", lambda e: e.activation(
            ss, ss, AF.Exp, bias=nm_), i=("sca", 1))
        v2 = sch.op("vector", lambda e: e.tensor_sub(aa_l, ss, aa_h),
                    w=[("sca", s2c)], i=("vec", 1))
        v3 = sch.op("vector", lambda e: e.reciprocal(ra_, se_),
                    w=[("sca", s1c), ("vec", v2)], i=("vec", 1))
        st["aa_ready"] = v2
        st["recip_ready"] = v3

    def b_av(st):
        nonlocal aa_guard_tmm, ss_free_vec
        aa_ready = st["aa_ready"]
        lvv = [0] * NTT
        ev_done = [0] * NTT
        avm_done = [0] * NTT

        def vt_load(kc):
            s_ = kc % 4
            war = avm_done[kc - 4] if kc >= 4 else vt_slot_free[s_]
            sch.op("sync", lambda e, kc=kc, s=s_: e.dma_start(
                out=vtb_h[s], in_=vts_h[kc]),
                w=[("tmm", war)], i=(f"lvt{s_}", 16))
            sch.op("sync", lambda e, kc=kc, s=s_: e.dma_start(
                out=vtb_l[s], in_=vts_l[kc]), i=(f"lvt{s_}", 16))
            lvv[kc] = sch.val(f"lvt{s_}")

        def tr(kc):
            ap_ = kc % 2
            bank_free = ev_done[kc - 2] if kc >= 2 else max(at_guard[ap_], trb_guard[ap_])
            sch.op("tensor", lambda e, kc=kc, p=ap_: e.transpose(
                ps_b[p][:, 0:128], aa_h[:, kc * 128:(kc + 1) * 128], identb[:]),
                w=[("vec", max(aa_ready, bank_free))])
            trv = sch.op("tensor", lambda e, kc=kc, p=ap_: e.transpose(
                ps_b[p][:, 128:256], aa_l[:, kc * 128:(kc + 1) * 128], identb[:]),
                i=("tmm", 1))
            slot_free = avm_done[kc - 2] if kc >= 2 else 0
            ev_done[kc] = sch.op("vector", lambda e, p=ap_: e.tensor_copy(
                at_sb[p], ps_b[p][:, 0:256]),
                w=[("tmm", max(trv, slot_free))], i=("vec", 1))

        def avmm(kc):
            ap_ = kc % 2
            s_ = kc % 4
            last = None
            for jh in range(2):
                a_h = at_sb[ap_][:, 0:128]
                a_l = at_sb[ap_][:, 128:256]
                m_h = vtb_h[s_][:, jh * 512:(jh + 1) * 512]
                m_l = vtb_l[s_][:, jh * 512:(jh + 1) * 512]
                w_ = []
                if jh == 0:
                    w_ = [("vec", max(ev_done[kc], av_guard) if kc == 0 else ev_done[kc]),
                          (f"lvt{s_}", lvv[kc])]
                f_ = (kc == 0)
                l_ = (kc == NTT - 1)
                sch.op("tensor", lambda e, a=a_h, b=m_h, jh=jh, f=f_: e.matmul(
                    ps_w[:, jh * 512:(jh + 1) * 512], a, b, start=f, stop=False), w=w_)
                sch.op("tensor", lambda e, a=a_h, b=m_l, jh=jh: e.matmul(
                    ps_w[:, jh * 512:(jh + 1) * 512], a, b, start=False, stop=False))
                last = sch.op("tensor", lambda e, a=a_l, b=m_h, jh=jh, l=l_: e.matmul(
                    ps_w[:, jh * 512:(jh + 1) * 512], a, b, start=False, stop=l),
                    i=("tmm", 1) if jh == 1 else None)
            avm_done[kc] = last

        for kc in range(3):
            vt_load(kc)
        tr(0)
        for kc in range(NTT):
            if kc + 3 < NTT:
                vt_load(kc + 3)
            if kc + 1 < NTT:
                tr(kc + 1)
            avmm(kc)
        at_guard[0] = ev_done[NTT - 2]
        at_guard[1] = ev_done[NTT - 1]
        trb_guard[0] = ev_done[NTT - 2]
        trb_guard[1] = ev_done[NTT - 1]
        for s_ in range(4):
            vt_slot_free[s_] = avm_done[NTT - 4 + s_]
        st["av_done"] = avm_done[NTT - 1]
        aa_guard_tmm = avm_done[NTT - 1]
        ss_free_vec = st["aa_ready"]

    def b_epilogue(st):
        nonlocal av_guard, xlt_free, gate_stat_sca, s_vec_guard
        qb, qpar, qc, tt = st["qb"], st["qpar"], st["qc"], st["tt"]
        ra_ = ra_p[qpar]
        v4 = sch.op("vector", lambda e, p=qpar: e.scalar_tensor_tensor(
            xt[p], ps_w[:, 0:1024], ra_, xt[p], ALU.mult, ALU.add),
            w=[("tmm", st["av_done"]), ("vec", st["recip_ready"]),
               (f"lx{qpar}", st["lxv"])], i=("vec", 1))
        av_guard = v4
        vdone, sdone = emit_ln(qpar, xtw=v4)
        for hc in range(HC):
            tp2 = hc % 2
            w_ = [("vec", trb_guard[tp2])]
            if hc == 0:
                w_ += [("sca", sdone), ("sxlt", xlt_free)]
            elif hc == 1:
                w_ += [("vec", vdone), ("sxlt", xlt_free)]
            sch.op("tensor", lambda e, hc=hc, p=tp2, q=qpar: e.transpose(
                ps_b[p][:, 0:128], xh[q][:, hc * 128:(hc + 1) * 128], identb[:]),
                w=w_)
            tv2 = sch.op("tensor", lambda e, hc=hc, p=tp2, q=qpar: e.transpose(
                ps_b[p][:, 128:256], xl[q][:, hc * 128:(hc + 1) * 128], identb[:]),
                w=[("vec", vdone)] if hc == 0 else [], i=("tmm", 1))
            sch.op("vector", lambda e, hc=hc, ti=tt, p=tp2: e.tensor_copy(
                xlt_h[:, hc * 512 + ti * 128: hc * 512 + (ti + 1) * 128],
                ps_b[p][:, 0:128]), w=[("tmm", tv2)], i=("vec", 1))
            vl2 = sch.op("vector", lambda e, hc=hc, ti=tt, p=tp2: e.tensor_copy(
                xlt_l[:, hc * 512 + ti * 128: hc * 512 + (ti + 1) * 128],
                ps_b[p][:, 128:256]), w=[("tmm", tv2)], i=("vec", 1))
            trb_guard[tp2] = vl2
        at_guard[0] = trb_guard[0]
        at_guard[1] = trb_guard[1]
        x2_evac = sch.val("vec")
        xhl_free_tmm[qpar] = sch.val("tmm")

        gmm = None
        for hc in range(HC):
            st_h = xlt_h[:, hc * 512 + tt * 128: hc * 512 + (tt + 1) * 128]
            st_l = xlt_l[:, hc * 512 + tt * 128: hc * 512 + (tt + 1) * 128]
            w_ = ([("vec", x2_evac), ("sca", s_guard[0])] if hc == 0 else [])
            sch.op("tensor", lambda e, a=st_h, hc=hc: e.matmul(
                ps_f[0][:, 0:8], a, gwT_h[:, hc * 8:(hc + 1) * 8],
                start=(hc == 0), stop=False), w=w_)
            sch.op("tensor", lambda e, a=st_h, hc=hc: e.matmul(
                ps_f[0][:, 0:8], a, gwT_l[:, hc * 8:(hc + 1) * 8],
                start=False, stop=False))
            gmm = sch.op("tensor", lambda e, a=st_l, hc=hc: e.matmul(
                ps_f[0][:, 0:8], a, gwT_h[:, hc * 8:(hc + 1) * 8],
                start=False, stop=(hc == HC - 1)),
                i=("tmm", 1) if hc == HC - 1 else None)

        m1 = stat[:, 44:45]
        m2 = stat[:, 45:46]
        dn = stat[:, 46:47]
        r2 = stat[:, 47:48]
        lc = cx[:, 0:8]
        ee = cx[:, 8:16]
        tmp = cx[:, 16:24]
        v5 = sch.op("vector", lambda e: e.reduce_max(m1, ps_f[0][:, 0:8], axis=AX.X),
                    w=[("tmm", gmm), ("sca", gate_stat_sca)], i=("vec", 1))
        v6 = sch.op("vector", lambda e: e.tensor_scalar_sub(lc, ps_f[0][:, 0:8], m1),
                    w=[("vec", v5)], i=("vec", 1))
        s3c = sch.op("scalar", lambda e: e.activation(ee, lc, AF.Exp),
                     w=[("vec", v6)], i=("sca", 1))
        v7 = sch.op("vector", lambda e: e.tensor_single_scalar(tmp, lc, 0.0, ALU.is_ge),
                    w=[("vec", v6)], i=("vec", 1))
        v8g = sch.op("vector", lambda e: e.scalar_tensor_tensor(
            tmp, tmp, -1e30, lc, ALU.mult, ALU.add), w=[("vec", v7)], i=("vec", 1))
        v9 = sch.op("vector", lambda e: e.reduce_max(m2, tmp, axis=AX.X),
                    w=[("vec", v8g)], i=("vec", 1))
        va_ = sch.op("vector", lambda e: e.tensor_single_scalar(tmp, lc, m2, ALU.is_ge),
                     w=[("vec", v9)], i=("vec", 1))
        s4c = sch.op("scalar", lambda e: e.activation(dn, m2, AF.Exp),
                     w=[("vec", v9)], i=("sca", 1))
        vb = sch.op("vector", lambda e: e.tensor_scalar_add(dn, dn, 1.0),
                    w=[("sca", s4c)], i=("vec", 1))
        vc = sch.op("vector", lambda e: e.reciprocal(r2, dn),
                    w=[("vec", vb)], i=("vec", 1))
        vd = sch.op("vector", lambda e: e.tensor_mul(tmp, tmp, ee),
                    w=[("sca", s3c), ("vec", va_)], i=("vec", 1))
        vee = sch.op("vector", lambda e, qb=qb: e.tensor_scalar_mul(
            comb[:, qb * 9: qb * 9 + 8], tmp, r2),
            w=[("vec", max(vc, vd))], i=("vec", 1))
        gate_stat_sca = s4c

        ct = sch.op("tensor", lambda e, qb=qb: e.transpose(
            ps_f[1][0:9, 0:128], comb[:, qb * 9: qb * 9 + 9], identf[:]),
            w=[("vec", vee), ("sca", s_guard[1]), ("gpc", gp_setup)], i=("tmm", 1))
        vf = sch.op("vector", lambda e: e.tensor_copy(cts[0:9, :], ps_f[1][0:9, 0:128]),
                    w=[("tmm", ct)], i=("vec", 1))
        bmm = None
        for jh in range(2):
            w_ = ([("vec", max(vf, av_guard)), ("lsetup", sch.val("lsetup"))]
                  if jh == 0 else [])
            bmm = sch.op("tensor", lambda e, jh=jh: e.matmul(
                ps_w[:, jh * 512:(jh + 1) * 512], cts[0:9, :],
                b2e[:, jh * 512:(jh + 1) * 512], start=True, stop=True),
                w=w_, i=("tmm", 1) if jh == 1 else None)
        vg_ = sch.op("vector", lambda e, p=qpar: e.tensor_add(
            xt[p], xt[p], ps_w[:, 0:1024]), w=[("tmm", bmm), ("vec", v4)], i=("vec", 1))
        av_guard = vg_
        xt_free_vec[qpar] = vg_
        sch.op("scalar", lambda e, qb=qb, p=qpar: e.dma_start(
            out=x1s[qb * 128:(qb + 1) * 128, :], in_=xt[p]),
            w=[("vec", vg_)], i=(f"sx1b{qpar}", 16))
        x1b_store[qpar] = sch.val(f"sx1b{qpar}")
        if tt == 3:
            sch.op("scalar", lambda e, qc=qc: e.dma_start(
                out=x2_h[qc], in_=xlt_h),
                w=[("vec", x2_evac), ("tmm", gmm)], i=("sxlt", 16))
            sch.op("scalar", lambda e, qc=qc: e.dma_start(
                out=x2_l[qc], in_=xlt_l),
                i=("sxlt", 16))
            xlt_free = sch.val("sxlt")
        s_guard[0] = 0
        s_guard[1] = 0
        s_vec_guard = max(s_vec_guard, vf)

    epi = None
    for qb in range(NTT):
        st = b_scores(qb)
        b_softmax(st)
        if epi is not None:
            b_epilogue(epi)
        b_av(st)
        epi = st
    b_epilogue(epi)

    if debug:
        sch.op("scalar", lambda e: e.dma_start(out=cmbs, in_=comb[:]),
               w=[("vec", sch.val("vec"))], i=("sxlt", 16))

    sch.barrier(extra_sync_waits=[("sx1b0", sch.val("sx1b0")),
                                  ("sx1b1", sch.val("sx1b1")),
                                  ("sxlt", sch.val("sxlt")),
                                  ("sev", sch.val("sev"))])

    ybank = [ps_f[2][:], ps_f[3][:], ps_w[:, 0:512], ps_w[:, 512:1024]]
    NIT = NE * 4
    w1_free = [0, 0]
    w1_ready = [0, 0]
    w2_free = [0, 0]
    w2_ready = [0, 0]
    x2_free = [0, 0]
    h_guard = [0, 0]
    y_guard = [0, 0, 0, 0]
    h_rdy_v = [0] * 8
    h_rdy_s = [0] * 8
    x1_st = 0
    h_cons = 0

    def w1_load_half(it, p_, half):
        e_i, fq = divmod(it, 4)
        ensure_pumped(it)
        mk = split_marks[it]
        srcs = w1s_h if half == 0 else w1s_l
        dst = w1h[p_] if half == 0 else w1l[p_]
        sch.op("sync", lambda e, e_i=e_i, fq=fq, d=dst: e.dma_start(
            out=d, in_=srcs[e_i, fq]),
            w=[("tmm", w1_free[p_]), ("ssw0", mk[0]), ("ssw1", mk[1])],
            i=(f"lw{p_}", 16))
        return sch.val(f"lw{p_}")

    def w2_load(it, jh):
        e_i, fq = divmod(it, 4)
        ensure_pumped(it)
        mk = split_marks[it]
        sch.op("sync", lambda e, e_i=e_i, fq=fq, jh=jh: e.dma_start(
            out=w2h[jh], in_=w2s_h[e_i, fq, jh]),
            w=[("tmm", w2_free[jh]), ("ssw0", mk[0]), ("ssw1", mk[1])],
            i=(f"lw2{jh}", 16))
        sch.op("sync", lambda e, e_i=e_i, fq=fq, jh=jh: e.dma_start(
            out=w2l[jh], in_=w2s_l[e_i, fq, jh]),
            i=(f"lw2{jh}", 16))
        return sch.val(f"lw2{jh}")

    w1_load_half(0, 0, 0)
    w1_ready[0] = w1_load_half(0, 0, 1)
    w1_load_half(1, 1, 0)
    w1_ready[1] = w1_load_half(1, 1, 1)
    w2_ready[0] = w2_load(0, 0)
    w2_ready[1] = w2_load(0, 1)

    _spl["stg"] = stg_c
    _spl["swh"], _spl["swl"] = swh_c, swl_c
    for it in range(NIT):
        e_i, fq = divmod(it, 4)
        wp = it % 2
        last_it = it == NIT - 1
        for t5 in range(NQC):
            xp = t5 % 2
            sch.op("sync", lambda e, t5=t5, p=xp: e.dma_start(
                out=xth[p], in_=x2_h[t5]),
                w=[("tmm", x2_free[xp])], i=(f"lx2{xp}", 16))
            sch.op("sync", lambda e, t5=t5, p=xp: e.dma_start(
                out=xtl[p], in_=x2_l[t5]),
                i=(f"lx2{xp}", 16))
            lx2v = sch.val(f"lx2{xp}")
            if it > 0:
                if t5 == 0:
                    w2_ready[1] = w2_load(it, 1)
                elif t5 == 2 and it + 1 < NIT:
                    w1_load_half(it + 1, (it + 1) % 2, 0)
                elif t5 == 4 and it + 1 < NIT:
                    w1_ready[(it + 1) % 2] = w1_load_half(it + 1, (it + 1) % 2, 1)
            sch.op("sync", lambda e, t5=t5: e.dma_start(
                out=x1t.rearrange("p (c d) -> p c d", c=4),
                in_=x1s.rearrange("(b p) d -> p b d", p=128)[:, t5 * 4:(t5 + 1) * 4, :]),
                w=[("sx1c", x1_st), ("vec", max(y_guard))], i=("lx1c", 16))
            lx1v = sch.val("lx1c")
            if it + 2 < NIT:
                pump_jobs(2)

            for fc in range(8):
                hp = fc % 2
                hmm = None
                for hc in range(HC):
                    st_h = w1h[wp][:, hc * 1024 + fc * 128: hc * 1024 + (fc + 1) * 128]
                    st_l = w1l[wp][:, hc * 1024 + fc * 128: hc * 1024 + (fc + 1) * 128]
                    mv_h = xth[xp][:, hc * 512:(hc + 1) * 512]
                    mv_l = xtl[xp][:, hc * 512:(hc + 1) * 512]
                    w_ = []
                    if hc == 0:
                        w_ = [("vec", h_guard[hp])]
                        if fc == 0:
                            w_ += [(f"lw{wp}", w1_ready[wp]), (f"lx2{xp}", lx2v),
                                   ("tmm", h_cons)]
                    sch.op("tensor", lambda e, a=st_h, b=mv_h, p=hp, f=(hc == 0): e.matmul(
                        ps_f[p][:], a, b, start=f, stop=False), w=w_)
                    sch.op("tensor", lambda e, a=st_h, b=mv_l, p=hp: e.matmul(
                        ps_f[p][:], a, b, start=False, stop=False))
                    hmm = sch.op("tensor", lambda e, a=st_l, b=mv_h, p=hp,
                                 l=(hc == HC - 1): e.matmul(
                        ps_f[p][:], a, b, start=False, stop=l),
                        i=("tmm", 1) if hc == HC - 1 else None)
                bias_col = b1T[:, e_i * 32 + fq * 8 + fc: e_i * 32 + fq * 8 + fc + 1]
                sv = sch.op("scalar", lambda e, p=hp, b=bias_col: e.activation(
                    h32, ps_f[p][:], AF.Gelu, bias=b),
                    w=[("tmm", hmm), ("vec", sch.val("vec"))], i=("sca", 1))
                sh = sch.op("scalar", lambda e, fc=fc: e.copy(
                    hh[:, fc * 512:(fc + 1) * 512], h32), w=[("sca", sv)], i=("sca", 1))
                vv = sch.op("vector", lambda e, fc=fc: e.tensor_sub(
                    hl[:, fc * 512:(fc + 1) * 512], h32, hh[:, fc * 512:(fc + 1) * 512]),
                    w=[("sca", sh)], i=("vec", 1))
                h_guard[hp] = vv
                h_rdy_v[fc] = vv
                h_rdy_s[fc] = sh

            for jh in range(2):
                for tb in range(4):
                    ymm = None
                    first_grp = (jh == 0 and tb == 0)
                    for fc in range(8):
                        a_h = hh[:, fc * 512 + tb * 128: fc * 512 + (tb + 1) * 128]
                        a_l = hl[:, fc * 512 + tb * 128: fc * 512 + (tb + 1) * 128]
                        m_h = w2h[jh][:, fc * 512:(fc + 1) * 512]
                        m_l = w2l[jh][:, fc * 512:(fc + 1) * 512]
                        w_ = []
                        if fc == 0:
                            w_ = [("vec", y_guard[tb]), (f"lw2{jh}", w2_ready[jh])]
                            if first_grp:
                                w_ += [("vec", h_rdy_v[6]), ("sca", h_rdy_s[6])]
                        elif fc == 7 and first_grp:
                            w_ = [("vec", h_rdy_v[7]), ("sca", h_rdy_s[7])]
                        sch.op("tensor", lambda e, a=a_h, b=m_h, tb=tb,
                               f=(fc == 0): e.matmul(
                            ybank[tb], a, b, start=f, stop=False), w=w_)
                        sch.op("tensor", lambda e, a=a_h, b=m_l, tb=tb: e.matmul(
                            ybank[tb], a, b, start=False, stop=False))
                        ymm = sch.op("tensor", lambda e, a=a_l, b=m_h, tb=tb,
                                     l=(fc == 7): e.matmul(
                            ybank[tb], a, b, start=False, stop=l),
                            i=("tmm", 1) if fc == 7 else None)
                    tt_g = t5 * 4 + tb
                    csc = comb[:, tt_g * 9 + e_i: tt_g * 9 + e_i + 1] if e_i < 8 else 1.0
                    dslc = x1t[:, tb * 1024 + jh * 512: tb * 1024 + (jh + 1) * 512]
                    vv = sch.op("vector", lambda e, tb=tb, c=csc, d=dslc: e.scalar_tensor_tensor(
                        d, ybank[tb], c, d, ALU.mult, ALU.add),
                        w=[("tmm", ymm), ("lx1c", lx1v)], i=("vec", 1))
                    y_guard[tb] = vv
                if jh == 0 and t5 == NQC - 1 and not last_it:
                    w2_free[0] = sch.val("tmm")
                    w2_ready[0] = w2_load(it + 1, 0)
            h_cons = sch.val("tmm")
            dst = out if last_it else x1s
            sch.op("scalar", lambda e, t5=t5, d=dst: e.dma_start(
                out=d.rearrange("(b p) d -> p b d", p=128)[:, t5 * 4:(t5 + 1) * 4, :],
                in_=x1t.rearrange("p (c d) -> p c d", c=4)),
                w=[("vec", sch.val("vec"))], i=("sx1c", 16))
            x1_st = sch.val("sx1c")
            x2_free[xp] = h_cons
        w1_free[wp] = sch.val("tmm")
        w2_free[1] = sch.val("tmm")

    sch.wait("sync", ("sx1c", sch.val("sx1c")))
    tgt = sch.cnt["bar"] + 1
    sch.op("sync", lambda e: e.nop(), i=("bar", 1))
    sch.wait("gpsimd", ("bar", tgt))

    blk = es.enter_context(nc.Block())

    @blk.tensor
    def _(e):
        for f in sch.q["tensor"]:
            f(e)

    @blk.vector
    def _(e):
        for f in sch.q["vector"]:
            f(e)

    @blk.scalar
    def _(e):
        for f in sch.q["scalar"]:
            f(e)

    @blk.gpsimd
    def _(e):
        for f in sch.q["gpsimd"]:
            f(e)

    @blk.sync
    def _(e):
        for f in sch.q["sync"]:
            f(e)

    es.close()
    return nc


def kernel(L0_attn_X, L0_attn_W_Q, L0_attn_W_K, L0_attn_W_V, L0_ffn_gate_w,
           L0_ffn_W1, L0_ffn_b1, L0_ffn_W2, L0_ffn_b2, L0_ffn_sW1, L0_ffn_sb1,
           L0_ffn_sW2, L0_ffn_sb2):
    nc = build_nc(debug=DBG)
    shared = {
        "wq": np.ascontiguousarray(L0_attn_W_Q, dtype=np.float32),
        "wk": np.ascontiguousarray(L0_attn_W_K, dtype=np.float32),
        "wv": np.ascontiguousarray(L0_attn_W_V, dtype=np.float32),
        "gw": np.ascontiguousarray(L0_ffn_gate_w, dtype=np.float32),
        "w1": np.ascontiguousarray(L0_ffn_W1, dtype=np.float32),
        "b1v": np.ascontiguousarray(L0_ffn_b1, dtype=np.float32),
        "w2": np.ascontiguousarray(L0_ffn_W2, dtype=np.float32),
        "b2v": np.ascontiguousarray(L0_ffn_b2, dtype=np.float32),
        "sw1": np.ascontiguousarray(L0_ffn_sW1, dtype=np.float32),
        "sb1v": np.ascontiguousarray(L0_ffn_sb1, dtype=np.float32),
        "sw2": np.ascontiguousarray(L0_ffn_sW2, dtype=np.float32),
        "sb2v": np.ascontiguousarray(L0_ffn_sb2, dtype=np.float32),
    }
    in_maps = []
    for c in range(8):
        m = dict(shared)
        m["x_in"] = np.ascontiguousarray(L0_attn_X[c], dtype=np.float32)
        in_maps.append(m)
    res = run_bass_kernel_spmd(nc, in_maps, core_ids=list(range(8)))
    return np.stack([np.asarray(r["out"]) for r in res.results], axis=0)
